# version 03
# baseline (speedup 1.0000x reference)
import numpy as np
import ml_dtypes
from contextlib import ExitStack
import concourse.bass as bass
import concourse.mybir as mybir
from concourse.bass_utils import run_bass_kernel_spmd

F32 = mybir.dt.float32
BF16 = mybir.dt.bfloat16
I32 = mybir.dt.int32
AF = mybir.ActivationFunctionType
ALU = mybir.AluOpType

T = 2048
D = 1024
KT = D // 128
NCHUNK = 512
LN_EPS = 1e-5


def _cc(a):
    return np.ascontiguousarray(a)


DEBUG_TAPS = False
NOFAST = False
F2_PHASES = 4
TZ_PHASE = 4


class Sem:
    def __init__(self, h):
        self.h = h
        self.n = 0

    def inc(self, instr, k=1):
        instr.then_inc(self.h, k)
        self.n += k
        return self.n

    def dma(self, instr):
        return self.inc(instr, 16)


class Prog:
    def __init__(self):
        self.nc = bass.Bass("TRN2", target_bir_lowering=False,
                            detect_race_conditions=False)
        self.es = ExitStack()
        self.scopes = []
        self.sems = {}
        self.bufs = {}

    def sem(self, name):
        if name not in self.sems:
            self.sems[name] = Sem(self.es.enter_context(self.nc.semaphore(name)))
        return self.sems[name]

    def sb(self, name, shape, dtype):
        if name not in self.bufs:
            self.bufs[name] = self.es.enter_context(
                self.nc.sbuf_tensor(name, list(shape), dtype))
        return self.bufs[name]

    def dram(self, name, shape, dtype, kind):
        key = "dram::" + name
        if key not in self.bufs:
            self.bufs[key] = self.nc.dram_tensor("hb_" + name, list(shape), dtype, kind=kind)
        return self.bufs[key]

    def push(self):
        self.scopes.append((ExitStack(), {}))

    def pop(self):
        es, _ = self.scopes.pop()
        es.close()

    def _scoped(self, name, mk):
        es, d = self.scopes[-1]
        if name not in d:
            d[name] = es.enter_context(mk())
        return d[name]

    def _uname(self, name):
        self._n = getattr(self, "_n", 0) + 1
        return f"{name}__{self._n}"

    def ssb(self, name, shape, dtype):
        return self._scoped(name, lambda: self.nc.sbuf_tensor(
            self._uname(name), list(shape), dtype))

    def sps(self, name, shape, dtype=F32):
        return self._scoped(name, lambda: self.nc.psum_tensor(
            self._uname(name), list(shape), dtype))

    def close(self):
        while self.scopes:
            self.pop()
        self.es.close()

    def tap(self, name, ap, rows=128):
        if not DEBUG_TAPS:
            return
        cols = ap.ap[-1][1] if hasattr(ap, "ap") else None
        d = self.nc.dram_tensor("tap_" + name, [rows, cols], ap.tensor.dtype,
                                kind="ExternalOutput")
        sd = self.sem("sdma")
        with self.nc.Block() as blk:
            @blk.sync
            def _(sy):
                sd.dma(sy.dma_start(d[:, :], ap))
                sy.wait_ge(sd.h, sd.n)



def emit_mm(p, jobs, uname, load_groups=None, loadsem="gdma", n_pp=2,
            psum_cols=NCHUNK):
    nc = p.nc
    pes = p.sem(f"pes_{uname}")
    frv = p.sem(f"frv_{uname}")
    frs = p.sem(f"frs_{uname}")
    gd = p.sem(loadsem)
    pss = [p.sps(f"ps_{uname}{i}", [128, psum_cols], F32) for i in range(n_pp)]
    base_p = pes.n

    ve_jobs = [(ji, j) for ji, j in enumerate(jobs)
               if j.get("evac_eng", "vector") == "vector"]
    se_jobs = [(ji, j) for ji, j in enumerate(jobs) if j.get("evac_eng") == "scalar"]
    evac_of = {}
    base_v = frv.n
    base_s = frs.n
    for idx, (ji, j) in enumerate(ve_jobs):
        evac_of[ji] = ("v", base_v + idx + 1)
    for idx, (ji, j) in enumerate(se_jobs):
        evac_of[ji] = ("s", base_s + idx + 1)

    with nc.Block() as blk:
        if load_groups:
            @blk.gpsimd
            def _(g):
                for need, pairs in load_groups:
                    if need:
                        g.wait_ge(pes.h, base_p + need)
                    for dst, src in pairs:
                        gd.dma(g.dma_start(dst, src))
                g.wait_ge(gd.h, gd.n)

        @blk.tensor
        def _(t):
            for ji, job in enumerate(jobs):
                if job.get("wait"):
                    semname, val = job["wait"]
                    t.wait_ge(p.sem(semname).h, val)
                if ji >= n_pp:
                    kind, val = evac_of[ji - n_pp]
                    t.wait_ge(frv.h if kind == "v" else frs.h, val)
                ps = pss[ji % n_pp]
                m, n = job["shape"]
                pa = ps[0:m, 0:n]
                ins = None
                for i, (lhsT, rhs) in enumerate(job["mms"]):
                    ins = t.matmul(pa, lhsT, rhs, start=(i == 0),
                                   stop=(i == len(job["mms"]) - 1))
                pes.inc(ins)

        def mk_body(sel_jobs, fre):
            def body(eng):
                for ji, job in sel_jobs:
                    eng.wait_ge(pes.h, base_p + ji + 1)
                    for sn, val in job.get("evac_waits") or []:
                        eng.wait_ge(p.sem(sn).h, val)
                    ps = pss[ji % n_pp]
                    m, n = job["shape"]
                    fre.inc(job["epi"](eng, ps[0:m, 0:n]))
            return body

        if ve_jobs:
            blk.vector(mk_body(ve_jobs, frv))
        if se_jobs:
            blk.scalar(mk_body(se_jobs, frs))


def proj_jobs(wtile, rhs_tile, out_write, M_total, nchunk=NCHUNK, ktiles=KT,
              extra_mms=None, ncols=T, evac_eng="vector"):
    jobs = []
    for m0 in range(0, M_total, 128):
        msz = min(128, M_total - m0)
        for n0 in range(0, ncols, nchunk):
            nsz = min(nchunk, ncols - n0)
            mms = [(wtile(k, m0, msz), rhs_tile(k, n0, nsz)) for k in range(ktiles)]
            if extra_mms:
                mms += extra_mms(m0, msz, n0, nsz)
            jobs.append(dict(
                shape=(msz, nsz), mms=mms, evac_eng=evac_eng,
                epi=(lambda eng, pa, m0=m0, msz=msz, n0=n0, nsz=nsz:
                     out_write(eng, pa, m0, msz, n0, nsz))))
    return jobs



def emit_ln(p, X, XN, apply_out=True, out_mu=None, out_rstd=None, sq=None):
    nc = p.nc
    ones1 = p.sb("c_ones1", [1, 128], BF16)
    ones128 = p.sb("c_ones128", [128, 1], BF16)

    p.push()
    s1 = p.ssb("row_s1", [1, T], F32)
    s2 = p.ssb("row_s2", [1, T], F32)
    tmp = p.ssb("row_tmp", [1, T], F32)
    rstd_bf = p.ssb("row_rstd_bf", [1, T], BF16)
    mup_bf = p.ssb("row_mup_bf", [1, T], BF16)
    SQ = sq if sq is not None else p.ssb("ln_SQ", [128, KT * T], BF16)
    with nc.Block() as blk:
        @blk.scalar
        def _(s):
            for j in range(KT):
                s.activation(SQ[:, j * T:(j + 1) * T], X[:, j * T:(j + 1) * T], AF.Square)
            for j in range(KT):
                s.activation(XN[:, j * T:(j + 1) * T], X[:, j * T:(j + 1) * T], AF.Copy)

    def sum_jobs(src, dst_row):
        jobs = []
        for n0 in range(0, T, NCHUNK):
            mms = [(ones128[:, 0:1], src[:, j * T + n0: j * T + n0 + NCHUNK])
                   for j in range(KT)]
            jobs.append(dict(
                shape=(1, NCHUNK), mms=mms,
                epi=(lambda eng, pa, n0=n0, dst_row=dst_row:
                     eng.tensor_copy(dst_row[0:1, n0:n0 + NCHUNK], pa))))
        return jobs

    p.push()
    emit_mm(p, sum_jobs(XN, s1) + sum_jobs(SQ, s2), uname="ln")
    p.pop()

    with nc.Block() as blk:
        @blk.vector
        def _(v):
            v.scalar_tensor_tensor(tmp[0:1, :], s1[0:1, :], 1.0 / D, s1[0:1, :],
                                   ALU.mult, ALU.mult)
            v.tensor_sub(tmp[0:1, :], s2[0:1, :], tmp[0:1, :])
            v.tensor_scalar(tmp[0:1, :], tmp[0:1, :], 1.0 / D, LN_EPS,
                            ALU.mult, ALU.add)
    with nc.Block() as blk:
        @blk.scalar
        def _(s):
            s.activation(tmp[0:1, :], tmp[0:1, :], AF.Sqrt)
    with nc.Block() as blk:
        @blk.vector
        def _(v):
            v.reciprocal(tmp[0:1, :], tmp[0:1, :])
    with nc.Block() as blk:
        @blk.vector
        def _(v):
            v.scalar_tensor_tensor(s2[0:1, :], s1[0:1, :], -1.0 / D, tmp[0:1, :],
                                   ALU.mult, ALU.mult)
            v.tensor_copy(rstd_bf[0:1, :], tmp[0:1, :])
            v.tensor_copy(mup_bf[0:1, :], s2[0:1, :])
            if out_mu is not None:
                v.tensor_scalar_mul(out_mu[0:1, :], s1[0:1, :], 1.0 / D)
            if out_rstd is not None:
                v.tensor_copy(out_rstd[0:1, :], tmp[0:1, :])
    if apply_out:
        ps_r = p.sps("ps_lnr", [128, T], F32)
        ps_m = p.sps("ps_lnm", [128, T], F32)
        with nc.Block() as blk:
            @blk.tensor
            def _(t):
                for n0 in range(0, T, NCHUNK):
                    t.matmul(ps_r[:, n0:n0 + NCHUNK], ones1[0:1, :],
                             rstd_bf[0:1, n0:n0 + NCHUNK], start=True, stop=True)
                    t.matmul(ps_m[:, n0:n0 + NCHUNK], ones1[0:1, :],
                             mup_bf[0:1, n0:n0 + NCHUNK], start=True, stop=True)
        with nc.Block() as blk:
            @blk.vector
            def _(v):
                for j in range(KT):
                    v.tensor_tensor(SQ[:, j * T:(j + 1) * T], X[:, j * T:(j + 1) * T],
                                    ps_r[:, :], ALU.mult)
        with nc.Block() as blk:
            @blk.vector
            def _(v):
                for j in range(KT):
                    v.tensor_tensor(XN[:, j * T:(j + 1) * T], SQ[:, j * T:(j + 1) * T],
                                    ps_m[:, :], ALU.add)
    p.pop()


def emit_load_x(p, X, x_in):
    nc = p.nc
    sd = p.sem("sdma")
    with nc.Block() as blk:
        @blk.sync
        def _(sy):
            for j in range(KT):
                sd.dma(sy.dma_start(X[:, j * T:(j + 1) * T],
                                    x_in[j * 128:(j + 1) * 128, :]))
            sy.wait_ge(sd.h, sd.n)


def emit_store_x(p, X, x_out):
    nc = p.nc
    sd = p.sem("sdma")
    with nc.Block() as blk:
        @blk.sync
        def _(sy):
            for j in range(KT):
                sd.dma(sy.dma_start(x_out[j * 128:(j + 1) * 128, :],
                                    X[:, j * T:(j + 1) * T]))
            sy.wait_ge(sd.h, sd.n)


def emit_eye(p, W):
    if getattr(p, "_eye_done", False):
        return
    p._eye_done = True
    nc = p.nc
    gd = p.sem("gdma")
    EYEB = p.sb("c_eyeb", [128, 128], BF16)
    EYEF = p.sb("c_eyef", [128, 128], F32)
    with nc.Block() as blk:
        @blk.gpsimd
        def _(g):
            gd.dma(g.dma_start(EYEB[:, :], W["c_eye"][:, :]))
            gd.dma(g.dma_start(EYEF[:, :], W["c_eye"][:, :]))
            g.wait_ge(gd.h, gd.n)


def emit_load_consts(p, W):
    nc = p.nc
    gd = p.sem("gdma")
    ones1 = p.sb("c_ones1", [1, 128], BF16)
    ones128 = p.sb("c_ones128", [128, 1], BF16)
    with nc.Block() as blk:
        @blk.gpsimd
        def _(g):
            gd.dma(g.dma_start(ones1[:, :], W["c_ones1"][:, :]))
            gd.dma(g.dma_start(ones128[:, :], W["c_ones128"][:, :]))
            g.wait_ge(gd.h, gd.n)




def emit_l0_attn(p, X, W):
    nc = p.nc
    gd = p.sem("gdma")
    TP = 2056

    p.push()
    XN = p.ssb("l0_XN", [128, KT * T], BF16)
    Z = p.ssb("l0_Z", [128, KT * T], BF16)
    PAR = p.ssb("l0_PAR", [64, T], BF16)
    PAR2 = p.ssb("l0_PAR2", [32, T], BF16)
    SML = p.ssb("l0_SML", [128, 64], F32)

    emit_ln(p, X, XN)
    p.tap("xn0", XN[:, 0:T])

    with nc.Block() as blk:
        @blk.gpsimd
        def _(g):
            gd.dma(g.dma_start(SML[:, 0:32], W["l0_cw_t"][:, :]))
            gd.dma(g.dma_start(SML[:, 32:40], W["l0_bdt_t"][:, :]))
            gd.dma(g.dma_start(SML[:, 40:48], W["l0_Dp_t"][:, :]))
            g.wait_ge(gd.h, gd.n)
        @blk.vector
        def _(v):
            v.memset(SML[:, 48:49], 1.0)

    p.push()
    XP = p.ssb("l0_XP", [128, KT * TP], BF16)
    WBUF = p.ssb("l0_WB", [128, 8 * 1024], BF16)
    CACC = p.ssb("l0_CACC", [128, T], F32)
    with nc.Block() as blk:
        @blk.vector
        def _(v):
            for j in range(KT):
                v.memset(XP[:, j * TP: j * TP + 3], 0.0)

    def load_w(name, wd=1024):
        with nc.Block() as blk:
            @blk.gpsimd
            def _(g):
                for k in range(KT):
                    gd.dma(g.dma_start(WBUF[:, k * wd:(k + 1) * wd],
                                       W[name][k * 128:(k + 1) * 128, :]))
                g.wait_ge(gd.h, gd.n)

    def wtile(k, m0, msz, wd=1024):
        return WBUF[:, k * wd + m0: k * wd + m0 + msz]

    def rhs_xn(k, n0, nsz):
        return XN[:, k * T + n0: k * T + n0 + nsz]

    load_w("l0_WxT")
    def xp_write(eng, pa, m0, msz, n0, nsz):
        j = m0 // 128
        return eng.activation(XP[:, j * TP + 3 + n0: j * TP + 3 + n0 + nsz], pa, AF.Copy)
    p.push()
    emit_mm(p, proj_jobs(wtile, rhs_xn, xp_write, D, evac_eng="scalar"), uname="l0a")
    p.pop()

    load_w("l0_WzT")
    def z_write(eng, pa, m0, msz, n0, nsz):
        j = m0 // 128
        return eng.activation(Z[:, j * T + n0: j * T + n0 + nsz], pa, AF.Silu)
    p.push()
    emit_mm(p, proj_jobs(wtile, rhs_xn, z_write, D, evac_eng="scalar"), uname="l0a")
    p.pop()

    load_w("l0_WpT", wd=96)
    def par_write(eng, pa, m0, msz, n0, nsz):
        return eng.tensor_copy(PAR[0:64, n0:n0 + nsz], pa)
    def par2_write(eng, pa, m0, msz, n0, nsz):
        return eng.tensor_copy(PAR2[0:32, n0:n0 + nsz], pa)
    p.push()
    jobs = proj_jobs(lambda k, m0, msz: wtile(k, 0, 64, wd=96), rhs_xn, par_write, 64)
    jobs += proj_jobs(lambda k, m0, msz: wtile(k, 64, 32, wd=96), rhs_xn, par2_write, 32)
    emit_mm(p, jobs, uname="l0a")
    p.pop()

    p.tap("xp0", XP[:, 0:2051])
    p.tap("z0", Z[:, 0:T])
    p.tap("par", PAR[0:64, 0:T], rows=64)
    p.tap("par2", PAR2[0:32, 0:T], rows=32)
    U = XN
    for j in range(KT):
        with nc.Block() as blk:
            @blk.vector
            def _(v):
                v.tensor_scalar_mul(CACC[:, :], XP[:, j * TP + 0: j * TP + 0 + T],
                                    SML[:, j * 4: j * 4 + 1])
                for kk in range(1, 4):
                    v.scalar_tensor_tensor(CACC[:, :],
                                           XP[:, j * TP + kk: j * TP + kk + T],
                                           SML[:, j * 4 + kk: j * 4 + kk + 1],
                                           CACC[:, :], ALU.mult, ALU.add)
        with nc.Block() as blk:
            @blk.scalar
            def _(s):
                s.activation(U[:, j * T:(j + 1) * T], CACC[:, :], AF.Silu)
    p.pop()

    p.push()
    DT = p.ssb("l0_DT", [128, KT * T], BF16)
    p.push()
    WDT = p.ssb("l0_WDT", [64, 1024], BF16)
    SPT = p.ssb("l0_SPT", [128, NCHUNK], F32)
    with nc.Block() as blk:
        @blk.gpsimd
        def _(g):
            gd.dma(g.dma_start(WDT[:, :], W["l0_WdtT"][:, :]))
            g.wait_ge(gd.h, gd.n)

    def dt_write(eng, pa, m0, msz, n0, nsz):
        j = m0 // 128
        eng.activation(SPT[:, 0:nsz], pa, AF.Exp, bias=SML[:, 32 + j: 33 + j])
        return eng.activation(DT[:, j * T + n0: j * T + n0 + nsz], SPT[:, 0:nsz],
                              AF.Ln, bias=SML[:, 48:49])
    p.push()
    emit_mm(p, proj_jobs(lambda k, m0, msz: WDT[:, m0:m0 + msz],
                         lambda k, n0, nsz: PAR[0:64, n0:n0 + nsz],
                         dt_write, D, ktiles=1, evac_eng="scalar"), uname="l0dt")
    p.pop()
    p.pop()

    p.tap("u0", U[:, 0:T])
    p.tap("dt0", DT[:, 0:T])
    p.push()
    ABUF = p.ssb("l0_ABUF", [128, 2 * 128], F32)
    AA = p.ssb("l0_a", [128, T], BF16)
    UB = p.ssb("l0_ub", [128, T], BF16)
    GG = p.ssb("l0_g", [128, T], BF16)
    YA = p.ssb("l0_y", [128, 2 * T], BF16)
    BC = p.ssb("l0_bc", [128, 4 * T], BF16)
    bc_dram = p.dram("l0_bcrow", [4, T], BF16, "Internal")
    acts = p.sem("l0_acts")
    dvs = p.sem("l0_dvs")

    with nc.Block() as blk:
        @blk.gpsimd
        def _(g):
            gd.dma(g.dma_start(ABUF[:, 0:128], W["l0_A_t"][:, :]))
            g.wait_ge(gd.h, gd.n)
    with nc.Block() as blk:
        @blk.scalar
        def _(s):
            s.activation(ABUF[:, 0:128], ABUF[:, 0:128], AF.Exp)
    with nc.Block() as blk:
        @blk.vector
        def _(v):
            v.tensor_scalar_mul(ABUF[:, 0:128], ABUF[:, 0:128], -1.0)
    with nc.Block() as blk:
        @blk.vector
        def _(v):
            v.tensor_scalar_add(ABUF[:, 128:256], ABUF[:, 0:128], 1e-8)
    with nc.Block() as blk:
        @blk.vector
        def _(v):
            v.reciprocal(ABUF[:, 128:256], ABUF[:, 128:256])

    def bounce(g, n, sl):
        gd.dma(g.dma_start(bc_dram[2 * sl: 2 * sl + 1, :], PAR2[n:n + 1, :]))
        gd.dma(g.dma_start(bc_dram[2 * sl + 1: 2 * sl + 2, :],
                           PAR2[16 + n:17 + n, :]))
        g.wait_ge(gd.h, gd.n)
        src_b = bass.AP(bc_dram.ap().tensor, 2 * sl * T, [[0, 128], [1, T]])
        src_c = bass.AP(bc_dram.ap().tensor, (2 * sl + 1) * T, [[0, 128], [1, T]])
        gd.dma(g.dma_start(BC[:, (2 * sl) * T:(2 * sl + 1) * T], src_b))
        gd.dma(g.dma_start(BC[:, (2 * sl + 1) * T:(2 * sl + 2) * T], src_c))
        g.wait_ge(gd.h, gd.n)

    npass = 4
    jper = KT // npass
    with nc.Block() as blk:
        @blk.gpsimd
        def _(g):
            bounce(g, 0, 0)

    chunk = 0
    a_base = acts.n
    d_base = dvs.n
    for pp in range(npass):
        for n in range(16):
            sl = n % 2
            with nc.Block() as blk:
                @blk.gpsimd
                def _(g):
                    if not (pp == npass - 1 and n == 15):
                        nxt = (n + 1) % 16
                        bounce(g, nxt, (sl + 1) % 2)

                @blk.scalar
                def _(s):
                    for jl in range(jper):
                        jj = pp * jper + jl
                        c = chunk + jl
                        if c > 0:
                            s.wait_ge(dvs.h, d_base + c)
                        acts.inc(s.activation(
                            AA[:, :], DT[:, jj * T:(jj + 1) * T], AF.Exp,
                            scale=ABUF[:, jj * 16 + n: jj * 16 + n + 1]))

                @blk.vector
                def _(v):
                    for jl in range(jper):
                        jj = pp * jper + jl
                        c = chunk + jl
                        v.wait_ge(acts.h, a_base + c + 1)
                        Bb = BC[:, (2 * sl) * T:(2 * sl + 1) * T]
                        Cb = BC[:, (2 * sl + 1) * T:(2 * sl + 2) * T]
                        v.tensor_tensor(UB[:, :], U[:, jj * T:(jj + 1) * T], Bb, ALU.mult)
                        v.scalar_tensor_tensor(UB[:, :], AA[:, :], -1.0, UB[:, :],
                                               ALU.add, ALU.mult)
                        v.tensor_tensor_scan(GG[:, :], AA[:, :], UB[:, :], 0.0,
                                             ALU.mult, ALU.add)
                        v.scalar_tensor_tensor(
                            UB[:, :], GG[:, :],
                            ABUF[:, 128 + jj * 16 + n: 129 + jj * 16 + n],
                            Cb, ALU.mult, ALU.mult)
                        if n == 0:
                            ins = v.tensor_copy(YA[:, jl * T:(jl + 1) * T], UB[:, :])
                        else:
                            ins = v.tensor_tensor(YA[:, jl * T:(jl + 1) * T],
                                                  YA[:, jl * T:(jl + 1) * T],
                                                  UB[:, :], ALU.add)
                        dvs.inc(ins)
            chunk += jper
        with nc.Block() as blk:
            @blk.vector
            def _(v):
                for jl in range(jper):
                    jj = pp * jper + jl
                    v.tensor_scalar_mul(UB[:, :], U[:, jj * T:(jj + 1) * T],
                                        SML[:, 40 + jj: 41 + jj])
                    v.tensor_tensor(UB[:, :], UB[:, :], YA[:, jl * T:(jl + 1) * T],
                                    ALU.add)
                    v.tensor_tensor(DT[:, jj * T:(jj + 1) * T], UB[:, :],
                                    Z[:, jj * T:(jj + 1) * T], ALU.mult)
    p.tap("y20", DT[:, 0:T])
    p.tap("abuf", ABUF[:, 0:256])
    p.pop()

    p.push()
    WO = p.ssb("l0_WO", [128, 8 * 1024], BF16)
    with nc.Block() as blk:
        @blk.gpsimd
        def _(g):
            for k in range(KT):
                gd.dma(g.dma_start(WO[:, k * 1024:(k + 1) * 1024],
                                   W["l0_WoT"][k * 128:(k + 1) * 128, :]))
            g.wait_ge(gd.h, gd.n)

    def res_write(eng, pa, m0, msz, n0, nsz):
        j = m0 // 128
        return eng.tensor_tensor(X[:, j * T + n0: j * T + n0 + nsz],
                                 X[:, j * T + n0: j * T + n0 + nsz], pa, ALU.add)
    emit_mm(p, proj_jobs(lambda k, m0, msz: WO[:, k * 1024 + m0: k * 1024 + m0 + msz],
                         lambda k, n0, nsz: DT[:, k * T + n0: k * T + n0 + nsz],
                         res_write, D), uname="l0a")
    p.pop()
    p.pop()
    p.pop()



def emit_l0_ffn(p, X, W):
    nc = p.nc
    gd = p.sem("gdma")
    emit_eye(p, W)
    EYEB = p.sb("c_eyeb", [128, 128], BF16)
    EYEF = p.sb("c_eyef", [128, 128], F32)

    p.push()
    XN = p.ssb("f0_XN", [128, KT * T], BF16)
    CF = p.ssb("f0_CF", [8, T], F32)
    CB = p.ssb("f0_CB", [128, 2 * T], BF16)
    cb_dram = p.dram("f0_cbrow", [2, T], F32, "Internal")
    p.push()
    MU = p.ssb("f0_MU", [1, T], F32)
    RSTD = p.ssb("f0_RSTD", [1, T], F32)
    emit_ln(p, X, XN, out_mu=MU, out_rstd=RSTD)

    GLB = p.ssb("f0_GLB", [8, T], F32)
    GT = p.ssb("f0_GT", [128, 128], F32)
    GMA = p.ssb("f0_GMA", [128, 352], F32)

    GW = p.ssb("f0_GW", [128, 8 * 8], F32)
    SGW = p.ssb("f0_SGW", [8, 9], F32)
    CS = p.ssb("f0_CS", [128, 2], F32)
    RSTtok = p.ssb("f0_RSTtok", [128, 16], F32)
    with nc.Block() as blk:
        @blk.gpsimd
        def _(g):
            for k in range(KT):
                gd.dma(g.dma_start(GW[:, k * 8:(k + 1) * 8],
                                   W["f0_gwT"][k * 128:(k + 1) * 128, :]))
            g.wait_ge(gd.h, gd.n)
        @blk.vector
        def _(v):
            v.memset(CS[:, :], 1.0)
    p.push()
    ps_g = p.sps("f0_psg", [128, NCHUNK], F32)
    with nc.Block() as blk:
        @blk.tensor
        def _(t):
            for k in range(KT):
                t.matmul(ps_g[0:8, 0:1], GW[:, k * 8:(k + 1) * 8], CS[:, 0:1],
                         start=(k == 0), stop=(k == KT - 1))
    with nc.Block() as blk:
        @blk.vector
        def _(v):
            v.tensor_copy(SGW[0:8, 0:1], ps_g[0:8, 0:1])
            v.tensor_scalar_mul(MU[0:1, :], MU[0:1, :], 1.0)
    with nc.Block() as blk:
        @blk.tensor
        def _(t):
            t.matmul(ps_g[0:1, 0:8], SGW[0:8, 0:1], EYEF[0:8, 0:8],
                     start=True, stop=True)
    with nc.Block() as blk:
        @blk.vector
        def _(v):
            v.tensor_scalar_mul(SGW[0:1, 1:9], ps_g[0:1, 0:8], -1.0)
    with nc.Block() as blk:
        @blk.tensor
        def _(t):
            for tt in range(16):
                t.matmul(ps_g[0:128, tt:tt + 1], RSTD[0:1, tt * 128:(tt + 1) * 128],
                         CS[0:1, 0:1], start=True, stop=True)
    with nc.Block() as blk:
        @blk.vector
        def _(v):
            v.tensor_copy(RSTtok[:, 0:16], ps_g[0:128, 0:16])
    p.pop()
    p.push()
    jobs = []
    for n0 in range(0, T, NCHUNK):
        mms = [(GW[:, k * 8:(k + 1) * 8], X[:, k * T + n0: k * T + n0 + NCHUNK])
               for k in range(KT)]
        mms.append((SGW[0:1, 1:9], MU[0:1, n0:n0 + NCHUNK]))
        jobs.append(dict(shape=(8, NCHUNK), mms=mms,
                         epi=(lambda eng, pa, n0=n0:
                              eng.tensor_copy(GLB[0:8, n0:n0 + NCHUNK], pa))))
    emit_mm(p, jobs, uname="f0g")
    p.pop()

    p.push()
    PT = p.sps("f0_pt", [128, 128], F32)
    with nc.Block() as blk:
        @blk.tensor
        def _(t):
            for tt in range(16):
                t.matmul(PT[:, tt * 8:(tt + 1) * 8], GLB[0:8, tt * 128:(tt + 1) * 128],
                         EYEF[0:8, 0:8], start=True, stop=True)
    with nc.Block() as blk:
        @blk.vector
        def _(v):
            v.tensor_copy(GT[:, :], PT[:, :])
    p.pop()

    CT = p.ssb("f0_CT", [128, 128], F32)
    with nc.Block() as blk:
        @blk.vector
        def _(v):
            for tt in range(16):
                v.reduce_max(GMA[:, tt:tt + 1], GT[:, 8 * tt:8 * tt + 8],
                             axis=mybir.AxisListType.X)
    with nc.Block() as blk:
        @blk.vector
        def _(v):
            for tt in range(16):
                v.tensor_scalar(GMA[:, 96 + 8 * tt:104 + 8 * tt],
                                GT[:, 8 * tt:8 * tt + 8],
                                GMA[:, tt:tt + 1], None, ALU.is_equal)
    with nc.Block() as blk:
        @blk.vector
        def _(v):
            for tt in range(16):
                v.scalar_tensor_tensor(GMA[:, 96 + 8 * tt:104 + 8 * tt],
                                       GMA[:, 96 + 8 * tt:104 + 8 * tt], -1e30,
                                       GT[:, 8 * tt:8 * tt + 8], ALU.mult, ALU.add)
    with nc.Block() as blk:
        @blk.vector
        def _(v):
            for tt in range(16):
                v.reduce_max(GMA[:, 16 + tt:17 + tt], GMA[:, 96 + 8 * tt:104 + 8 * tt],
                             axis=mybir.AxisListType.X)
    with nc.Block() as blk:
        @blk.vector
        def _(v):
            v.tensor_sub(GMA[:, 32:48], GMA[:, 16:32], GMA[:, 0:16])
    with nc.Block() as blk:
        @blk.vector
        def _(v):
            v.tensor_tensor(GMA[:, 32:48], GMA[:, 32:48], RSTtok[:, 0:16], ALU.mult)
    with nc.Block() as blk:
        @blk.scalar
        def _(s):
            s.activation(GMA[:, 32:48], GMA[:, 32:48], AF.Exp)
    with nc.Block() as blk:
        @blk.vector
        def _(v):
            v.tensor_scalar_add(GMA[:, 32:48], GMA[:, 32:48], 1.0)
    with nc.Block() as blk:
        @blk.vector
        def _(v):
            v.reciprocal(GMA[:, 48:64], GMA[:, 32:48])
    with nc.Block() as blk:
        @blk.vector
        def _(v):
            v.tensor_scalar(GMA[:, 64:80], GMA[:, 48:64], -1.0, 1.0,
                            ALU.mult, ALU.add)
    with nc.Block() as blk:
        @blk.vector
        def _(v):
            for tt in range(16):
                v.tensor_scalar(GMA[:, 96 + 8 * tt:104 + 8 * tt],
                                GT[:, 8 * tt:8 * tt + 8],
                                GMA[:, tt:tt + 1], None, ALU.is_equal)
            for tt in range(16):
                v.tensor_scalar(GMA[:, 224 + 8 * tt:232 + 8 * tt],
                                GT[:, 8 * tt:8 * tt + 8],
                                GMA[:, 16 + tt:17 + tt], None, ALU.is_equal)
    with nc.Block() as blk:
        @blk.vector
        def _(v):
            for tt in range(16):
                v.tensor_scalar(GMA[:, 96 + 8 * tt:104 + 8 * tt],
                                GMA[:, 96 + 8 * tt:104 + 8 * tt],
                                GMA[:, 48 + tt:49 + tt], None, ALU.mult)
    with nc.Block() as blk:
        @blk.vector
        def _(v):
            for tt in range(16):
                v.scalar_tensor_tensor(CT[:, 8 * tt:8 * tt + 8],
                                       GMA[:, 224 + 8 * tt:232 + 8 * tt],
                                       GMA[:, 64 + tt:65 + tt],
                                       GMA[:, 96 + 8 * tt:104 + 8 * tt],
                                       ALU.mult, ALU.add)

    p.tap("gt", GT[:, 0:128])
    p.tap("gma", GMA[:, 0:160])
    p.tap("ct", CT[:, 0:128])
    p.tap("rsttok", RSTtok[:, 0:16])
    p.push()
    PT2 = p.sps("f0_pt2", [8, T], F32)
    with nc.Block() as blk:
        @blk.tensor
        def _(t):
            for tt in range(16):
                t.matmul(PT2[0:8, tt * 128:(tt + 1) * 128], CT[:, 8 * tt:8 * tt + 8],
                         EYEF[:, :], start=True, stop=True)
    with nc.Block() as blk:
        @blk.vector
        def _(v):
            v.tensor_copy(CF[0:8, :], PT2[0:8, :])
    p.pop()

    p.tap("glb", GLB[0:8, 0:T], rows=8)
    p.tap("cf", CF[0:8, 0:T], rows=8)
    p.pop()
    H = p.ssb("f0_H", [128, 32 * 1024], BF16)
    W1B = p.ssb("f0_W1", [128, 4 * 1024], BF16)
    W2B = p.ssb("f0_W2", [128, 2 * 512], BF16)
    pes2 = p.sem("f0_pes2")

    def bounce_c(g, e, sl):
        gd.dma(g.dma_start(cb_dram[sl:sl + 1, :], CF[e:e + 1, :]))
        g.wait_ge(gd.h, gd.n)
        srcp = bass.AP(cb_dram.ap().tensor, sl * T, [[0, 128], [1, T]])
        gd.dma(g.dma_start(CB[:, sl * T:(sl + 1) * T], srcp))
        g.wait_ge(gd.h, gd.n)

    with nc.Block() as blk:
        @blk.gpsimd
        def _(g):
            bounce_c(g, 0, 0)

    for tc in range(2):
        t0 = tc * 1024
        for e in range(8):
            sl = e % 2
            jobs = []
            load_groups = []
            base_gd = gd.n
            for mt in range(32):
                wsl = mt % 2
                need = (2 * (mt - 1)) if mt >= 2 else None
                load_groups.append((need, [
                    (W1B[:, wsl * 2048: wsl * 2048 + 1024],
                     W[f"f0_w1T_{e}"][:, mt * 1024:(mt + 1) * 1024])]))
                for half in range(2):
                    n0 = t0 + half * 512
                    jobs.append(dict(
                        shape=(128, 512), evac_eng="scalar",
                        mms=[(W1B[:, wsl * 2048 + k * 128: wsl * 2048 + (k + 1) * 128],
                              XN[:, k * T + n0: k * T + n0 + 512]) for k in range(KT)],
                        epi=(lambda eng, pa, mt=mt, half=half:
                             eng.activation(H[:, mt * 1024 + half * 512:
                                              mt * 1024 + (half + 1) * 512],
                                            pa, AF.Relu)),
                        wait=("gdma", base_gd + 16 * (mt + 1)),
                    ))
            p.push()
            emit_mm(p, jobs, uname="f0h", load_groups=load_groups, n_pp=4)
            p.pop()

            for dp in range(2):
                p.push()
                pss = [p.sps(f"f0_y{i}", [128, 512], F32) for i in range(8)]
                base_pe = pes2.n
                base_gd2 = gd.n
                with nc.Block() as blk:
                    @blk.gpsimd
                    def _(g):
                        for k in range(32):
                            if k >= 2:
                                g.wait_ge(pes2.h, base_pe + (k - 1))
                            gd.dma(g.dma_start(
                                W2B[:, (k % 2) * 512:(k % 2 + 1) * 512],
                                W[f"f0_w2T_{e}"][k * 128:(k + 1) * 128,
                                                 dp * 512:(dp + 1) * 512]))
                        g.wait_ge(gd.h, gd.n)
                        if dp == 1 and not (tc == 1 and e == 7):
                            ne = e + 1 if e < 7 else 0
                            bounce_c(g, ne, (sl + 1) % 2)

                    @blk.tensor
                    def _(t):
                        for k in range(32):
                            t.wait_ge(gd.h, base_gd2 + 16 * (k + 1))
                            ins = None
                            for dt4 in range(4):
                                for half in range(2):
                                    ins = t.matmul(
                                        pss[dt4 * 2 + half][:, :],
                                        W2B[:, (k % 2) * 512 + dt4 * 128:
                                            (k % 2) * 512 + (dt4 + 1) * 128],
                                        H[:, k * 1024 + half * 512:
                                          k * 1024 + (half + 1) * 512],
                                        start=(k == 0), stop=(k == 31))
                            pes2.inc(ins)

                    @blk.vector
                    def _(v):
                        v.wait_ge(pes2.h, base_pe + 32)
                        for dt4 in range(4):
                            dt = dp * 4 + dt4
                            for half in range(2):
                                n0 = t0 + half * 512
                                v.tensor_tensor(pss[dt4 * 2 + half][:, :],
                                                pss[dt4 * 2 + half][:, :],
                                                CB[:, sl * T + n0: sl * T + n0 + 512],
                                                ALU.mult)
                                v.tensor_tensor(X[:, dt * T + n0: dt * T + n0 + 512],
                                                X[:, dt * T + n0: dt * T + n0 + 512],
                                                pss[dt4 * 2 + half][:, :], ALU.add)
                p.pop()
    p.pop()



def _ternarize(p, W, src_name, rows, cols, scl, rs, tern_dram, gd):
    nc = p.nc
    emit_eye(p, W)
    EYEB = p.sb("c_eyeb", [128, 128], BF16)
    nrt = rows // 128
    ndt = cols // 128
    p.push()
    WTMP = p.ssb("tz_w", [128, cols], F32)
    ABSW = p.ssb("tz_a", [128, cols], F32)
    TRN = p.ssb("tz_t", [128, cols], BF16)
    TST = p.ssb("tz_s", [128, cols], BF16)
    pes = p.sem("tz_pes")
    for rt in range(nrt):
        with nc.Block() as blk:
            @blk.gpsimd
            def _(g):
                gd.dma(g.dma_start(WTMP[:, :],
                                   W[src_name][rt * 128:(rt + 1) * 128, :]))
                g.wait_ge(gd.h, gd.n)
        with nc.Block() as blk:
            @blk.scalar
            def _(s):
                s.activation(ABSW[:, :], WTMP[:, :], AF.Abs)
        with nc.Block() as blk:
            @blk.vector
            def _(v):
                v.reduce_sum(scl[:, rt:rt + 1], ABSW[:, :], axis=mybir.AxisListType.X)
        with nc.Block() as blk:
            @blk.vector
            def _(v):
                v.tensor_scalar(scl[:, rt:rt + 1], scl[:, rt:rt + 1], 1.0 / cols,
                                1e-5, ALU.mult, ALU.max)
        with nc.Block() as blk:
            @blk.vector
            def _(v):
                v.reciprocal(rs[:, rt:rt + 1], scl[:, rt:rt + 1])
        if TZ_PHASE <= 1:
            continue
        with nc.Block() as blk:
            @blk.vector
            def _(v):
                v.tensor_scalar(ABSW[:, :], WTMP[:, :],
                                rs[:, rt:rt + 1], None, ALU.mult)
                v.tensor_single_scalar(WTMP[:, :], ABSW[:, :], 0.5, ALU.is_gt)
                v.tensor_single_scalar(ABSW[:, :], ABSW[:, :], -0.5, ALU.is_lt)
                v.tensor_sub(TRN[:, :], WTMP[:, :], ABSW[:, :])
        if TZ_PHASE <= 2:
            continue
        p.push()
        PST = p.sps("tz_ps", [128, 128], BF16)
        tzv = p.sem("tz_frv")
        base_pe = pes.n
        base_v = tzv.n
        with nc.Block() as blk:
            @blk.tensor
            def _(t):
                for dt in range(ndt):
                    if dt >= 1:
                        t.wait_ge(tzv.h, base_v + dt)
                    pes.inc(t.transpose(PST[:, :],
                                        TRN[:, dt * 128:(dt + 1) * 128], EYEB[:, :]))
            @blk.vector
            def _(v):
                for dt in range(ndt):
                    v.wait_ge(pes.h, base_pe + dt + 1)
                    tzv.inc(v.tensor_copy(TST[:, dt * 128:(dt + 1) * 128],
                                          PST[:, :]))
        p.pop()
        if TZ_PHASE <= 3:
            continue
        with nc.Block() as blk:
            @blk.gpsimd
            def _(g):
                dst = bass.AP(tern_dram.ap().tensor, rt * 128,
                              [[rows, 128], [128 * rows, ndt], [1, 128]])
                gd.dma(g.dma_start(dst, TST[:, 0:cols]))
                g.wait_ge(gd.h, gd.n)
    p.pop()


def emit_l2_ffn(p, X, W):
    nc = p.nc
    gd = p.sem("gdma")
    HID = 4096
    p.push()
    XN = p.ssb("f2_XN", [128, KT * T], BF16)
    emit_ln(p, X, XN)

    SCLF = p.ssb("f2_sclf", [128, 32], F32)
    RSF = p.ssb("f2_rsf", [128, 32], F32)
    SCLP = p.ssb("f2_sclp", [128, 8], F32)
    RSP = p.ssb("f2_rsp", [128, 8], F32)
    GOS = p.ssb("f2_gos", [128, 34], F32)
    Z12 = p.ssb("f2_z", [16, 2 * T], BF16)
    UPT = p.ssb("f2_upt", [16, HID], BF16)
    VW = p.ssb("f2_vw", [128, 8 * 16], BF16)
    FBW = p.ssb("f2_fbw", [16, 16], BF16)
    H = p.ssb("f2_H", [128, 32 * 512], BF16)
    FCB = p.ssb("f2_fcb", [128, 2 * 1024], BF16)
    WPB = p.ssb("f2_wpb", [128, 2 * 1024], BF16)
    tern_fc = p.dram("f2_ternfc", [D, HID], BF16, "Internal")
    tern_wp = p.dram("f2_ternwp", [HID, D], BF16, "Internal")
    _ternarize(p, W, "f2_wfc", HID, D, SCLF, RSF, tern_fc, gd)
    _ternarize(p, W, "f2_wproj", D, HID, SCLP, RSP, tern_wp, gd)
    if F2_PHASES <= 1:
        p.pop(); return

    with nc.Block() as blk:
        @blk.gpsimd
        def _(g):
            for k in range(KT):
                gd.dma(g.dma_start(VW[:, k * 16:(k + 1) * 16],
                                   W["f2_vT"][k * 128:(k + 1) * 128, :]))
            gd.dma(g.dma_start(FBW[:, :], W["f2_fbT"][:, :]))
            srcg = bass.AP(W["f2_gate"].ap().tensor, 0, [[0, 128], [1, 1]])
            gd.dma(g.dma_start(GOS[:, 32:33], srcg))
            g.wait_ge(gd.h, gd.n)
    with nc.Block() as blk:
        @blk.vector
        def _(v):
            v.tensor_scalar(GOS[:, 0:32], RSF[:, 0:32], GOS[:, 32:33], None, ALU.mult)
    p.push()
    emit_mm(p, proj_jobs(lambda k, m0, msz: VW[:, k * 16:(k + 1) * 16],
                         lambda k, n0, nsz: XN[:, k * T + n0: k * T + n0 + nsz],
                         lambda eng, pa, m0, msz, n0, nsz:
                         eng.tensor_copy(Z12[0:16, n0:n0 + nsz], pa), 16), uname="f2z")
    p.pop()
    p.push()
    jobs = [dict(shape=(16, NCHUNK),
                 mms=[(FBW[0:16, 0:16], Z12[0:16, n0:n0 + NCHUNK])],
                 epi=(lambda eng, pa, n0=n0:
                      eng.tensor_copy(Z12[0:16, T + n0: T + n0 + NCHUNK], pa)))
            for n0 in range(0, T, NCHUNK)]
    emit_mm(p, jobs, uname="f2z")
    p.pop()
    p.push()
    UTMP = p.ssb("f2_ut", [128, 16], F32)
    UTB = p.ssb("f2_utb", [128, 16], BF16)
    PSU = p.sps("f2_psu", [16, 128], F32)
    EYEB2 = p.sb("c_eyeb", [128, 128], BF16)
    for ft in range(32):
        with nc.Block() as blk:
            @blk.gpsimd
            def _(g):
                gd.dma(g.dma_start(UTMP[:, 0:16],
                                   W["f2_u"][ft * 128:(ft + 1) * 128, :]))
                g.wait_ge(gd.h, gd.n)
        with nc.Block() as blk:
            @blk.vector
            def _(v):
                v.tensor_scalar(UTB[:, 0:16], UTMP[:, 0:16],
                                GOS[:, ft:ft + 1], None, ALU.mult)
        with nc.Block() as blk:
            @blk.tensor
            def _(t):
                t.matmul(PSU[0:16, 0:128], UTB[:, 0:16], EYEB2[:, :],
                         start=True, stop=True)
        with nc.Block() as blk:
            @blk.vector
            def _(v):
                v.tensor_copy(UPT[0:16, ft * 128:(ft + 1) * 128], PSU[0:16, 0:128])
    p.pop()

    p.tap("xn2", XN[:, 0:T])
    p.tap("xn2f", XN[:, 0:KT * T])
    p.tap("sclf", SCLF[:, 0:32])
    p.tap("z12", Z12[0:16, 0:2 * T], rows=16)
    p.tap("upt", UPT[0:16, 0:HID], rows=16)
    if DEBUG_TAPS:
        with nc.Block() as blk:
            @blk.gpsimd
            def _(g):
                srcdbg = bass.AP(tern_fc.ap().tensor, 0,
                                 [[HID, 128], [128 * HID, 8], [1, 128]])
                gd.dma(g.dma_start(FCB[:, 0:1024], srcdbg))
                g.wait_ge(gd.h, gd.n)
        p.tap("fcb0", FCB[:, 0:1024])
    if F2_PHASES <= 2:
        p.pop(); return
    pes2 = p.sem("f2_pes2")
    for tc in range(4):
        t0 = tc * 512
        jobs = []
        load_groups = []
        base_gd = gd.n
        for mt in range(32):
            wsl = mt % 2
            need = (mt - 1) if mt >= 2 else None
            src = bass.AP(tern_fc.ap().tensor, mt * 128,
                          [[HID, 128], [128 * HID, 8], [1, 128]])
            load_groups.append((need, [
                (FCB[:, wsl * 1024: wsl * 1024 + 1024], src)]))
            jobs.append(dict(
                shape=(128, 512), evac_eng="vector",
                mms=[(FCB[:, wsl * 1024 + k * 128: wsl * 1024 + k * 128 + 128],
                      XN[:, k * T + t0: k * T + t0 + 512]) for k in range(KT)]
                    + ([] if NOFAST else [(UPT[0:16, mt * 128:(mt + 1) * 128],
                        Z12[0:16, T + t0: T + t0 + 512])]),
                epi=(lambda eng, pa, mt=mt:
                     eng.tensor_scalar(H[:, mt * 512:(mt + 1) * 512], pa,
                                       SCLF[:, mt:mt + 1], 0.0, ALU.mult, ALU.max)),
                wait=("gdma", base_gd + 16 * (mt + 1)),
            ))
        p.push()
        emit_mm(p, jobs, uname="f2h", load_groups=load_groups, n_pp=4)
        p.pop()
        if tc == 0:
            p.tap("h0", H[:, 0:512])
        with nc.Block() as blk:
            @blk.scalar
            def _(s):
                for mt in range(32):
                    s.activation(H[:, mt * 512:(mt + 1) * 512],
                                 H[:, mt * 512:(mt + 1) * 512], AF.Square)
        if F2_PHASES <= 3:
            continue
        p.push()
        pss = [p.sps(f"f2_y{i}", [128, 512], F32) for i in range(8)]
        base_pe = pes2.n
        base_gd2 = gd.n
        with nc.Block() as blk:
            @blk.gpsimd
            def _(g):
                for k in range(32):
                    if k >= 2:
                        g.wait_ge(pes2.h, base_pe + (k - 1))
                    gd.dma(g.dma_start(WPB[:, (k % 2) * 1024:(k % 2 + 1) * 1024],
                                       tern_wp[k * 128:(k + 1) * 128, :]))
                g.wait_ge(gd.h, gd.n)

            @blk.tensor
            def _(t):
                for k in range(32):
                    t.wait_ge(gd.h, base_gd2 + 16 * (k + 1))
                    ins = None
                    for dt in range(KT):
                        ins = t.matmul(pss[dt][:, :],
                                       WPB[:, (k % 2) * 1024 + dt * 128:
                                           (k % 2) * 1024 + (dt + 1) * 128],
                                       H[:, k * 512:(k + 1) * 512],
                                       start=(k == 0), stop=(k == 31))
                    pes2.inc(ins)

            @blk.vector
            def _(v):
                v.wait_ge(pes2.h, base_pe + 32)
                for dt in range(KT):
                    v.scalar_tensor_tensor(X[:, dt * T + t0: dt * T + t0 + 512],
                                           pss[dt][:, :], SCLP[:, dt:dt + 1],
                                           X[:, dt * T + t0: dt * T + t0 + 512],
                                           ALU.mult, ALU.add)
        p.pop()
    p.pop()



def emit_shift(p, Y, B1):
    nc = p.nc
    with nc.Block() as blk:
        @blk.vector
        def _(v):
            for j in range(KT):
                if j < 4:
                    v.tensor_copy(B1[:, j * T:(j + 1) * T], Y[:, j * T:(j + 1) * T])
                else:
                    v.memset(B1[:, j * T: j * T + 1], 0.0)
                    v.tensor_copy(B1[:, j * T + 1:(j + 1) * T],
                                  Y[:, j * T: (j + 1) * T - 1])


def emit_l1_ffn_layer(p, Y, W, li, X=None):
    nc = p.nc
    gd = p.sem("gdma")
    ones1 = p.sb("c_ones1", [1, 128], BF16)
    p.push()
    B1 = p.ssb("f1_B1", [128, KT * T], BF16)
    AT2 = p.ssb("f1_AT2", [128, KT * T], BF16)
    LGB = p.ssb("f1_lgb", [128, 96], F32)
    B1A = p.ssb("f1_b1a", [128, 48], F32)
    B2R = p.ssb("f1_b2t", [128, 8], F32)
    ONR = p.ssb("f1_onr", [1, 512], BF16)
    p.push()
    TAB = p.ssb("f1_tab", [128, 4 * T], BF16)
    QSC = p.ssb("f1_qsc", [64, 16], F32)
    MSKL = p.ssb("f1_mskl", [128, 4 * 256], BF16)
    WH = p.ssb("f1_wh", [128, KT * 192], BF16)
    QA = p.ssb("f1_qa", [65, T], BF16)
    KA = p.ssb("f1_ka", [65, T], BF16)
    VT = p.ssb("f1_vt", [128, 16 * 64], BF16)
    QR = p.ssb("f1_qr", [64, T], BF16)
    T1 = p.ssb("f1_t1", [64, T], BF16)
    T2 = p.ssb("f1_t2", [64, T], BF16)
    RQ = p.ssb("f1_rq", [1, T], BF16)
    RQF = p.ssb("f1_rqf", [1, T], F32)

    with nc.Block() as blk:
        @blk.gpsimd
        def _(g):
            for nm, c0 in [("f1_csq", 0), ("f1_ssq", T), ("f1_csk", 2 * T),
                           ("f1_ssk", 3 * T)]:
                gd.dma(g.dma_start(TAB[0:64, c0:c0 + T], W[nm][:, :]))
            gd.dma(g.dma_start(QSC[0:64, 0:6], W["f1_qs"][:, :]))
            gd.dma(g.dma_start(QSC[0:64, 8:14], W["f1_ks"][:, :]))
            gd.dma(g.dma_start(LGB[:, 0:48], W["f1_lng"][:, :]))
            gd.dma(g.dma_start(LGB[:, 48:96], W["f1_lnb"][:, :]))
            gd.dma(g.dma_start(B1A[:, 0:22], W[f"f1_b1g_{li}"][:, :]))
            gd.dma(g.dma_start(B1A[:, 24:46], W[f"f1_b1a_{li}"][:, :]))
            gd.dma(g.dma_start(B2R[:, :], W[f"f1_b2_{li}"][:, :]))
            gd.dma(g.dma_start(MSKL[:, :], W["f1_mskl"][:, :]))
            g.wait_ge(gd.h, gd.n)
        @blk.vector
        def _(v):
            v.memset(ONR[0:1, :], 1.0)

    emit_shift(p, Y, B1)

    for h in range(16):
        with nc.Block() as blk:
            @blk.gpsimd
            def _(g):
                for part, off in [(0, 0), (1, 64), (2, 128)]:
                    srca = bass.AP(W[f"f1_wqkv_{li}"].ap().tensor,
                                   part * 1024 + h * 64,
                                   [[3072, 128], [128 * 3072, 8], [1, 64]])
                    gd.dma(g.dma_start(
                        bass.AP(WH.ap().tensor, off,
                                [[KT * 192, 128], [192, 8], [1, 64]]), srca))
                g.wait_ge(gd.h, gd.n)
        for dst, off in [(QR, 0), (T1, 64)]:
            p.push()
            jobs = []
            for n0 in range(0, T, NCHUNK):
                mms = [(WH[:, k * 192 + off: k * 192 + off + 64],
                        B1[:, k * T + n0: k * T + n0 + NCHUNK]) for k in range(KT)]
                jobs.append(dict(
                    shape=(64, NCHUNK), mms=mms, evac_eng="scalar",
                    epi=(lambda eng, pa, n0=n0, dst=dst:
                         eng.activation(dst[0:64, n0:n0 + NCHUNK], pa, AF.Copy))))
            emit_mm(p, jobs, uname="f1p")
            p.pop()
        p.push()
        jobs = []
        for tt in range(16):
            mms = [(B1[:, k * T + tt * 128: k * T + (tt + 1) * 128],
                    WH[:, k * 192 + 128: k * 192 + 192]) for k in range(KT)]
            jobs.append(dict(
                shape=(128, 64), mms=mms, evac_eng="scalar",
                epi=(lambda eng, pa, tt=tt:
                     eng.activation(VT[:, tt * 64:(tt + 1) * 64], pa, AF.Copy))))
        emit_mm(p, jobs, uname="f1v")
        p.pop()

        for SRC, DST, tb, sccol in [(QR, QA, 0, 0), (T1, KA, 2 * T, 8)]:
            p.push()
            NPS = p.sps("f1_nps", [64, T], F32)
            with nc.Block() as blk:
                @blk.scalar
                def _(s):
                    s.activation(T2[0:64, :], SRC[0:64, :], AF.Square)
            p.push()
            jobs = []
            for n0 in range(0, T, NCHUNK):
                jobs.append(dict(
                    shape=(1, NCHUNK),
                    mms=[(p.sb("c_ones128", [128, 1], BF16)[0:64, 0:1],
                          T2[0:64, n0:n0 + NCHUNK])],
                    epi=(lambda eng, pa, n0=n0:
                         eng.tensor_copy(RQF[0:1, n0:n0 + NCHUNK], pa))))
            emit_mm(p, jobs, uname="f1n")
            p.pop()
            with nc.Block() as blk:
                @blk.scalar
                def _(s):
                    s.activation(RQF[0:1, :], RQF[0:1, :], AF.Sqrt)
            with nc.Block() as blk:
                @blk.vector
                def _(v):
                    v.tensor_scalar_max(RQF[0:1, :], RQF[0:1, :], 1e-12)
            with nc.Block() as blk:
                @blk.vector
                def _(v):
                    v.reciprocal(RQF[0:1, :], RQF[0:1, :])
            with nc.Block() as blk:
                @blk.vector
                def _(v):
                    v.tensor_copy(RQ[0:1, :], RQF[0:1, :])
            with nc.Block() as blk:
                @blk.tensor
                def _(t):
                    for n0 in range(0, T, NCHUNK):
                        t.matmul(NPS[0:64, n0:n0 + NCHUNK], ones1[0:1, 0:64],
                                 RQ[0:1, n0:n0 + NCHUNK], start=True, stop=True)
            with nc.Block() as blk:
                @blk.vector
                def _(v):
                    v.tensor_tensor(SRC[0:64, :], SRC[0:64, :], NPS[0:64, :], ALU.mult)
            with nc.Block() as blk:
                @blk.vector
                def _(v):
                    v.tensor_scalar(SRC[0:64, :], SRC[0:64, :],
                                    QSC[0:64, sccol + li:sccol + li + 1], None, ALU.mult)
            with nc.Block() as blk:
                @blk.vector
                def _(v):
                    v.tensor_tensor(T2[0:64, :], SRC[0:64, :],
                                    TAB[0:64, tb:tb + T], ALU.mult)
                    v.tensor_tensor(DST[0:32, :], SRC[32:64, :],
                                    TAB[32:64, tb + T:tb + T + T], ALU.mult)
                    v.tensor_tensor(DST[32:64, :], SRC[0:32, :],
                                    TAB[0:32, tb + T:tb + T + T], ALU.mult)
            with nc.Block() as blk:
                @blk.vector
                def _(v):
                    v.tensor_tensor(DST[0:64, :], DST[0:64, :], T2[0:64, :], ALU.add)
            p.pop()

        EYEB_ = p.sb("c_eyeb", [128, 128], BF16)
        p.push()
        CPS = p.sps("f1_cps", [128, NCHUNK], F32)
        CPR = p.sps("f1_cpr", [1, 128], F32)
        CMX = p.ssb("f1_cmx", [128, 2], BF16)
        for w in range(8):
            for r in range(2):
                ks0 = max(0, (w - 1) * 256 + r * 128)
                ks1 = w * 256 + r * 128 + 128
                with nc.Block() as blk:
                    @blk.tensor
                    def _(t):
                        t.matmul(CPS[:, 0:ks1 - ks0],
                                 QA[0:64, w * 256 + r * 128: w * 256 + r * 128 + 128],
                                 KA[0:64, ks0:ks1], start=True, stop=True)
                with nc.Block() as blk:
                    @blk.vector
                    def _(v):
                        v.reduce_max(CMX[:, 0:1], CPS[:, 0:ks1 - ks0],
                                     axis=mybir.AxisListType.X)
                with nc.Block() as blk:
                    @blk.tensor
                    def _(t):
                        t.matmul(CPR[0:1, :], CMX[:, 0:1], EYEB_[:, :],
                                 start=True, stop=True)
                with nc.Block() as blk:
                    @blk.vector
                    def _(v):
                        v.tensor_scalar_mul(
                            QA[64:65, w * 256 + r * 128: w * 256 + r * 128 + 128],
                            CPR[0:1, :], -1.0)
        p.pop()
        with nc.Block() as blk:
            @blk.vector
            def _(v):
                v.memset(KA[64:65, :], 1.0)

        def wout(eng, ot, w, h=h):
            base = (h % 2) * 64
            j = h // 2
            return eng.tensor_copy(
                AT2[base:base + 64, j * T + w * 256: j * T + (w + 1) * 256], ot)

        attn_core(p, QA, KA, VT, wout, list(range(8)),
                  lambda w: list(range(max(0, 2 * w - 2), 2 * w + 2)),
                  lambda w, tk: MSKL[:, (tk - (2 * w - 2)) * 256:
                                     (tk - (2 * w - 2) + 1) * 256],
                  uname="f1a", qcw=256)

    p.tap("f1at2", AT2[:, 0:T])
    p.pop()
    p.push()
    WOB = p.ssb("f1_wob", [128, 1024], BF16)
    for dt in range(KT):
        with nc.Block() as blk:
            @blk.gpsimd
            def _(g):
                gd.dma(g.dma_start(WOB[:, :],
                                   W[f"f1_wo_{li}"][:, dt * 1024:(dt + 1) * 1024]))
                g.wait_ge(gd.h, gd.n)
        p.push()
        jobs = []
        for n0 in range(0, T, NCHUNK):
            mms = [(WOB[:, k * 128:(k + 1) * 128],
                    AT2[:, k * T + n0: k * T + n0 + NCHUNK]) for k in range(KT)]
            jobs.append(dict(
                shape=(128, NCHUNK), mms=mms,
                epi=(lambda eng, pa, n0=n0, dt=dt:
                     eng.tensor_tensor(Y[:, dt * T + n0: dt * T + n0 + NCHUNK],
                                       Y[:, dt * T + n0: dt * T + n0 + NCHUNK],
                                       pa, ALU.add))))
        emit_mm(p, jobs, uname="f1o")
        p.pop()
    p.pop()

    emit_shift(p, Y, B1)
    B3 = p.ssb("f1_B3", [128, KT * T], BF16)
    emit_ln(p, B1, B3, sq=AT2)
    with nc.Block() as blk:
        @blk.vector
        def _(v):
            for j in range(KT):
                v.tensor_scalar(B3[:, j * T:(j + 1) * T], B3[:, j * T:(j + 1) * T],
                                LGB[:, li * 8 + j:li * 8 + j + 1],
                                LGB[:, 48 + li * 8 + j:49 + li * 8 + j],
                                ALU.mult, ALU.add)
    p.push()
    H = p.ssb("f1_H", [128, 22 * 256], BF16)
    GEL = p.ssb("f1_gel", [128, 2 * 256], BF16)
    W1B = p.ssb("f1_w1b", [128, 4 * 1024], BF16)
    W2B = p.ssb("f1_w2b", [128, 2 * 1024], BF16)
    pesd = p.sem("f1_pesd")
    for tc in range(8):
        t0 = tc * 256
        jobs = []
        load_groups = []
        base_gd = gd.n
        frs_now = p.sem("frs_f1h").n
        frv_now = p.sem("frv_f1h").n
        for mt in range(22):
            wsl = mt % 2
            need = (2 * (mt - 1)) if mt >= 2 else None
            load_groups.append((need, [
                (W1B[:, wsl * 2048: wsl * 2048 + 1024],
                 W[f"f1_w1g_{li}"][:, mt * 1024:(mt + 1) * 1024]),
                (W1B[:, wsl * 2048 + 1024: wsl * 2048 + 2048],
                 W[f"f1_w1a_{li}"][:, mt * 1024:(mt + 1) * 1024]),
            ]))
            gate_ew = ([("frv_f1h", frv_now + mt)] if mt >= 1 else None)
            jobs.append(dict(
                shape=(128, 256), evac_eng="scalar",
                mms=[(W1B[:, wsl * 2048 + k * 128: wsl * 2048 + (k + 1) * 128],
                      B3[:, k * T + t0: k * T + t0 + 256]) for k in range(KT)],
                epi=(lambda eng, pa, mt=mt:
                     eng.activation(GEL[:, (mt % 2) * 256:(mt % 2 + 1) * 256],
                                    pa, AF.Gelu, bias=B1A[:, mt:mt + 1])),
                wait=("gdma", base_gd + 32 * (mt + 1)),
                evac_waits=gate_ew,
            ))
            jobs.append(dict(
                shape=(128, 256), evac_eng="vector",
                mms=[(W1B[:, wsl * 2048 + 1024 + k * 128:
                          wsl * 2048 + 1024 + (k + 1) * 128],
                      B3[:, k * T + t0: k * T + t0 + 256]) for k in range(KT)],
                epi=(lambda eng, pa, mt=mt:
                     eng.scalar_tensor_tensor(H[:, mt * 256:(mt + 1) * 256], pa,
                                              B1A[:, 24 + mt:25 + mt],
                                              GEL[:, (mt % 2) * 256:(mt % 2 + 1) * 256],
                                              ALU.add, ALU.mult)),
                evac_waits=[("frs_f1h", frs_now + mt + 1)],
            ))
        p.push()
        emit_mm(p, jobs, uname="f1h", load_groups=load_groups, n_pp=4, psum_cols=256)
        p.pop()

        p.push()
        pss = [p.sps(f"f1_y{i}", [128, 256], F32) for i in range(8)]
        base_pe = pesd.n
        base_gd2 = gd.n
        with nc.Block() as blk:
            @blk.gpsimd
            def _(g):
                for k in range(22):
                    if k >= 2:
                        g.wait_ge(pesd.h, base_pe + (k - 1))
                    gd.dma(g.dma_start(W2B[:, (k % 2) * 1024:(k % 2 + 1) * 1024],
                                       W[f"f1_w2_{li}"][k * 128:(k + 1) * 128, :]))
                g.wait_ge(gd.h, gd.n)

            @blk.tensor
            def _(t):
                for k in range(22):
                    t.wait_ge(gd.h, base_gd2 + 16 * (k + 1))
                    ins = None
                    for dt in range(KT):
                        ins = t.matmul(pss[dt][:, :],
                                       W2B[:, (k % 2) * 1024 + dt * 128:
                                           (k % 2) * 1024 + (dt + 1) * 128],
                                       H[:, k * 256:(k + 1) * 256],
                                       start=(k == 0), stop=(k == 21))
                    pesd.inc(ins)

            @blk.vector
            def _(v):
                v.wait_ge(pesd.h, base_pe + 22)
                for dt in range(KT):
                    v.scalar_tensor_tensor(Y[:, dt * T + t0: dt * T + t0 + 256],
                                           pss[dt][:, :], B2R[:, dt:dt + 1],
                                           Y[:, dt * T + t0: dt * T + t0 + 256],
                                           ALU.add, ALU.add)
        p.pop()
    p.pop()
    if X is not None:
        with nc.Block() as blk:
            @blk.vector
            def _(v):
                for j in range(KT):
                    v.tensor_tensor(X[:, j * T:(j + 1) * T], X[:, j * T:(j + 1) * T],
                                    Y[:, j * T:(j + 1) * T], ALU.add)
    p.pop()



def attn_core(p, QA, KA, VT, out_write, qc_list, live_fn, mask_fn, uname,
              dv=64, qcw=512):
    nc = p.nc
    p.push()
    ones128 = p.sb("c_ones128", [128, 1], BF16)
    ones1 = p.sb("c_ones1", [1, 128], BF16)
    PSA = [p.sps(f"{uname}_a{i}", [128, qcw], F32) for i in range(2)]
    PSO = p.sps(f"{uname}_o", [dv, qcw], F32)
    PSZ = p.sps(f"{uname}_z", [1, qcw], F32)
    PSB = p.sps(f"{uname}_b", [dv, qcw], F32)
    EB = p.ssb(f"{uname}_eb", [128, 2 * qcw], BF16)
    OT = p.ssb(f"{uname}_ot", [dv, qcw], BF16)
    RZ = p.ssb(f"{uname}_rz", [1, qcw], BF16)
    RZF = p.ssb(f"{uname}_rzf", [1, qcw], F32)
    ape = p.sem(f"{uname}_pe")
    aac = p.sem(f"{uname}_ac")
    amv = p.sem(f"{uname}_mv")

    for qc in qc_list:
        q0 = qc * qcw
        live = live_fn(qc)
        n = len(live)
        masks = [mask_fn(qc, tk) for tk in live]
        base_pe = ape.n
        base_ac = aac.n
        base_mv = amv.n
        nmask = 0
        mask_cnt = []
        for m in masks:
            if m is not None:
                nmask += 1
            mask_cnt.append(nmask)
        with nc.Block() as blk:
            @blk.tensor
            def _(t):
                for i, tk in enumerate(live):
                    if i >= 2:
                        t.wait_ge(aac.h, base_ac + i - 1)
                    ins = t.matmul(PSA[i % 2][:, :], KA[0:65, tk * 128:(tk + 1) * 128],
                                   QA[0:65, q0:q0 + qcw], start=True, stop=True)
                    ape.inc(ins)
                    if i >= 1:
                        j = i - 1
                        if masks[j] is not None:
                            t.wait_ge(amv.h, base_mv + mask_cnt[j])
                        else:
                            t.wait_ge(aac.h, base_ac + j + 1)
                        t.matmul(PSO[0:dv, :], VT[:, live[j] * dv:(live[j] + 1) * dv],
                                 EB[:, (j % 2) * qcw:(j % 2 + 1) * qcw],
                                 start=(j == 0), stop=(j == n - 1))
                        ins = t.matmul(PSZ[0:1, :], ones128[:, 0:1],
                                       EB[:, (j % 2) * qcw:(j % 2 + 1) * qcw],
                                       start=(j == 0), stop=(j == n - 1))
                j = n - 1
                if masks[j] is not None:
                    t.wait_ge(amv.h, base_mv + mask_cnt[j])
                else:
                    t.wait_ge(aac.h, base_ac + j + 1)
                t.matmul(PSO[0:dv, :], VT[:, live[j] * dv:(live[j] + 1) * dv],
                         EB[:, (j % 2) * qcw:(j % 2 + 1) * qcw],
                         start=(j == 0), stop=(j == n - 1))
                ins = t.matmul(PSZ[0:1, :], ones128[:, 0:1],
                               EB[:, (j % 2) * qcw:(j % 2 + 1) * qcw],
                               start=(j == 0), stop=(j == n - 1))
                ape.inc(ins)

            @blk.scalar
            def _(s):
                for i, tk in enumerate(live):
                    s.wait_ge(ape.h, base_pe + i + 1)
                    ins = s.activation(EB[:, (i % 2) * qcw:(i % 2 + 1) * qcw],
                                       PSA[i % 2][:, :], AF.Exp)
                    aac.inc(ins)

            @blk.vector
            def _(v):
                for i, tk in enumerate(live):
                    if masks[i] is not None:
                        v.wait_ge(aac.h, base_ac + i + 1)
                        ins = v.tensor_tensor(EB[:, (i % 2) * qcw:(i % 2 + 1) * qcw],
                                              EB[:, (i % 2) * qcw:(i % 2 + 1) * qcw],
                                              masks[i], ALU.mult)
                        amv.inc(ins)
        with nc.Block() as blk:
            @blk.vector
            def _(v):
                v.reciprocal(RZF[0:1, :], PSZ[0:1, :])
                v.tensor_copy(OT[0:dv, :], PSO[0:dv, :])
        with nc.Block() as blk:
            @blk.vector
            def _(v):
                v.tensor_copy(RZ[0:1, :], RZF[0:1, :])
        with nc.Block() as blk:
            @blk.tensor
            def _(t):
                t.matmul(PSB[0:dv, :], ones1[0:1, 0:dv], RZ[0:1, :],
                         start=True, stop=True)
        with nc.Block() as blk:
            @blk.vector
            def _(v):
                v.tensor_tensor(OT[0:dv, :], OT[0:dv, :], PSB[0:dv, :], ALU.mult)
        with nc.Block() as blk:
            @blk.vector
            def _(v):
                out_write(v, OT[0:dv, :], qc)
    p.pop()


def emit_cbound(p, QA, KA, uname, scale, sqs_buf=None):
    nc = p.nc
    ones128 = p.sb("c_ones128", [128, 1], BF16)
    p.push()
    SQS = sqs_buf if sqs_buf is not None else p.ssb(f"{uname}_sqs", [64, T], BF16)
    KMX = p.ssb(f"{uname}_kmx", [2, 2], F32)
    ctx_lp = p.nc.allow_low_precision("c-bound is an upper bound with large slack")
    ctx_lp.__enter__()
    for SRC in [QA, KA]:
        with nc.Block() as blk:
            @blk.scalar
            def _(s):
                s.activation(SQS[0:64, :], SRC[0:64, :], AF.Square)
        p.push()
        jobs = []
        for n0 in range(0, T, NCHUNK):
            jobs.append(dict(
                shape=(1, NCHUNK),
                mms=[(ones128[0:64, 0:1], SQS[0:64, n0:n0 + NCHUNK])],
                epi=(lambda eng, pa, n0=n0, SRC=SRC:
                     eng.tensor_copy(SRC[64:65, n0:n0 + NCHUNK], pa))))
        emit_mm(p, jobs, uname=f"{uname}_nm")
        p.pop()
    with nc.Block() as blk:
        @blk.scalar
        def _(s):
            s.activation(QA[64:65, :], QA[64:65, :], AF.Sqrt)
            s.activation(KA[64:65, :], KA[64:65, :], AF.Sqrt)
    with nc.Block() as blk:
        @blk.vector
        def _(v):
            v.reduce_max(KMX[0:1, 1:2], KA[64:65, :], axis=mybir.AxisListType.X)
    with nc.Block() as blk:
        @blk.vector
        def _(v):
            v.tensor_scalar(QA[64:65, :], QA[64:65, :], KMX[0:1, 1:2], None, ALU.mult)
    with nc.Block() as blk:
        @blk.vector
        def _(v):
            v.tensor_scalar_mul(QA[64:65, :], QA[64:65, :], -1.0)
            v.memset(KA[64:65, :], 1.0)
    ctx_lp.__exit__(None, None, None)
    p.pop()



def emit_l1_attn(p, X, W):
    nc = p.nc
    gd = p.sem("gdma")
    ones1 = p.sb("c_ones1", [1, 128], BF16)
    SH = 0.125
    p.push()
    ATTO = p.ssb("a1_atto", [128, KT * T], BF16)
    p.push()
    XN = p.ssb("a1_XN", [128, KT * T], BF16)
    emit_ln(p, X, XN)
    WH = p.ssb("a1_WH", [128, KT * 320], BF16)
    BRR = p.ssb("a1_BR", [1, 16 * 320 + 1024 + 8], BF16)
    ONR = p.ssb("a1_onr", [1, 512], BF16)
    LAMN = p.ssb("a1_lam", [128, 1], F32)
    QA = p.ssb("a1_qa", [65, 2 * T], BF16)
    KA = p.ssb("a1_ka", [65, 2 * T], BF16)
    VT = p.ssb("a1_vt", [128, 16 * 64], BF16)
    O1B = p.ssb("a1_o1", [64, 2 * T], BF16)

    with nc.Block() as blk:
        @blk.gpsimd
        def _(g):
            gd.dma(g.dma_start(BRR[0:1, 0:16 * 320 + 1024], W["a1_brow"][:, :]))
            srcl = bass.AP(W["a1_lam"].ap().tensor, 0, [[0, 128], [1, 1]])
            gd.dma(g.dma_start(LAMN[:, :], srcl))
            g.wait_ge(gd.h, gd.n)
        @blk.vector
        def _(v):
            v.memset(ONR[0:1, :], 1.0)
    with nc.Block() as blk:
        @blk.vector
        def _(v):
            v.tensor_scalar_mul(LAMN[:, :], LAMN[:, :], -1.0)

    for h in range(16):
        with nc.Block() as blk:
            @blk.gpsimd
            def _(g):
                for k in range(KT):
                    gd.dma(g.dma_start(WH[:, k * 320:(k + 1) * 320],
                                       W[f"a1_wh_{h}"][k * 128:(k + 1) * 128, :]))
                g.wait_ge(gd.h, gd.n)
        bb = h * 320
        for st in range(2):
            for dst, cb, bc, sc in [(QA, st * 64, bb + st * 64, 1.0),
                                    (KA, 128 + st * 64, bb + 128 + st * 64, SH)]:
                p.push()
                jobs = []
                for n0 in range(0, T, NCHUNK):
                    mms = [(WH[:, k * 320 + cb: k * 320 + cb + 64],
                            XN[:, k * T + n0: k * T + n0 + NCHUNK]) for k in range(KT)]
                    mms.append((BRR[0:1, bc:bc + 64], ONR[0:1, 0:NCHUNK]))
                    jobs.append(dict(
                        shape=(64, NCHUNK), mms=mms, evac_eng="scalar",
                        epi=(lambda eng, pa, n0=n0, dst=dst, sc=sc, st=st:
                             eng.activation(dst[0:64, st * T + n0: st * T + n0 + NCHUNK],
                                            pa, AF.Copy, scale=sc))))
                emit_mm(p, jobs, uname="a1p")
                p.pop()
        p.push()
        jobs = []
        for tt in range(16):
            mms = [(XN[:, k * T + tt * 128: k * T + (tt + 1) * 128],
                    WH[:, k * 320 + 256: k * 320 + 320]) for k in range(KT)]
            mms.append((ONR[0:1, 0:128], BRR[0:1, bb + 256: bb + 320]))
            jobs.append(dict(
                shape=(128, 64), mms=mms, evac_eng="scalar",
                epi=(lambda eng, pa, tt=tt:
                     eng.activation(VT[:, tt * 64:(tt + 1) * 64], pa, AF.Copy))))
        emit_mm(p, jobs, uname="a1v")
        p.pop()

        for st in range(2):
            QAv = QA[0:65, st * T:(st + 1) * T]
            KAv = KA[0:65, st * T:(st + 1) * T]
            emit_cbound(p, QAv, KAv, "a1c", SH)

            def wout(eng, ot, qc, st=st):
                return eng.tensor_copy(
                    O1B[0:64, st * T + qc * 512: st * T + (qc + 1) * 512], ot)

            attn_core(p, QAv, KAv, VT, wout, list(range(4)),
                      lambda qc: list(range(16)), lambda qc, tk: None, uname="a1")
        base = (h % 2) * 64
        j = h // 2
        with nc.Block() as blk:
            @blk.vector
            def _(v):
                v.scalar_tensor_tensor(ATTO[base:base + 64, j * T:(j + 1) * T],
                                       O1B[0:64, T:2 * T], LAMN[0:64, 0:1],
                                       O1B[0:64, 0:T], ALU.mult, ALU.add)
    p.pop()

    p.push()
    Y = p.ssb("a1_Y", [128, KT * T], BF16)
    YN = p.ssb("a1_YN", [128, KT * T], BF16)
    WOB = p.ssb("a1_wob", [128, 1024], BF16)
    GB = p.ssb("a1_gb", [128, 24], F32)
    with nc.Block() as blk:
        @blk.gpsimd
        def _(g):
            gd.dma(g.dma_start(GB[:, 0:8], W["a1_gamma_t"][:, :]))
            gd.dma(g.dma_start(GB[:, 8:16], W["a1_beta_t"][:, :]))
            g.wait_ge(gd.h, gd.n)
    with nc.Block() as blk:
        @blk.vector
        def _(v):
            v.tensor_scalar_mul(GB[:, 0:16], GB[:, 0:16], 0.2)
    for dt in range(KT):
        with nc.Block() as blk:
            @blk.gpsimd
            def _(g):
                gd.dma(g.dma_start(WOB[:, :], W["a1_WoC"][:, dt * 1024:(dt + 1) * 1024]))
                g.wait_ge(gd.h, gd.n)
        p.push()
        jobs = []
        for n0 in range(0, T, NCHUNK):
            mms = [(WOB[:, k * 128:(k + 1) * 128],
                    ATTO[:, k * T + n0: k * T + n0 + NCHUNK]) for k in range(KT)]
            mms.append((BRR[0:1, 16 * 320 + dt * 128: 16 * 320 + (dt + 1) * 128],
                        ONR[0:1, 0:NCHUNK]))
            jobs.append(dict(
                shape=(128, NCHUNK), mms=mms, evac_eng="scalar",
                epi=(lambda eng, pa, n0=n0, dt=dt:
                     eng.activation(Y[:, dt * T + n0: dt * T + n0 + NCHUNK],
                                    pa, AF.Copy))))
        emit_mm(p, jobs, uname="a1o")
        p.pop()
    emit_ln(p, Y, YN, sq=ATTO)
    with nc.Block() as blk:
        @blk.vector
        def _(v):
            for jj in range(KT):
                v.tensor_scalar(YN[:, jj * T:(jj + 1) * T], YN[:, jj * T:(jj + 1) * T],
                                GB[:, jj:jj + 1], GB[:, 8 + jj:9 + jj],
                                ALU.mult, ALU.add)
    with nc.Block() as blk:
        @blk.vector
        def _(v):
            for jj in range(KT):
                v.tensor_tensor(X[:, jj * T:(jj + 1) * T], X[:, jj * T:(jj + 1) * T],
                                YN[:, jj * T:(jj + 1) * T], ALU.add)
    p.pop()
    p.pop()



def emit_l2_attn(p, X, W):
    nc = p.nc
    gd = p.sem("gdma")
    ones1 = p.sb("c_ones1", [1, 128], BF16)
    p.push()
    XN = p.ssb("a2_XN", [128, KT * T], BF16)
    emit_ln(p, X, XN)
    WA = p.ssb("a2_WA", [128, KT * 3072], BF16)
    BR = p.ssb("a2_BR", [1, 4096 + 8], BF16)
    ONR = p.ssb("a2_onr", [1, 512], BF16)
    ATTO = p.ssb("a2_atto", [128, KT * T], BF16)
    QA = p.ssb("a2_qa", [65, T], BF16)
    KA = p.ssb("a2_ka", [65, T], BF16)
    VT = p.ssb("a2_vt", [128, 16 * 64], BF16)
    MSK = p.ssb("a2_msk", [128, 4 * 512], BF16)
    p.push()
    CSE = p.ssb("a2_cse", [128, 2 * 512], BF16)
    SCR = p.ssb("a2_scr", [128, 4 * 512], BF16)

    with nc.Block() as blk:
        @blk.gpsimd
        def _(g):
            for k in range(KT):
                gd.dma(g.dma_start(WA[:, k * 3072:(k + 1) * 3072],
                                   W["a2_Wattn"][k * 128:(k + 1) * 128, :]))
            gd.dma(g.dma_start(BR[0:1, 0:4096], W["a2_battn"][:, :]))
            gd.dma(g.dma_start(CSE[:, 0:512], W["a2_cose"][:, :]))
            gd.dma(g.dma_start(CSE[:, 512:1024], W["a2_sine"][:, :]))
            gd.dma(g.dma_start(MSK[:, :], W["a2_mskc"][:, :]))
            g.wait_ge(gd.h, gd.n)
        @blk.vector
        def _(v):
            v.memset(ONR[0:1, :], 1.0)

    def ev(tens, base, cnt=512, parts=(0, 128)):
        return bass.AP(tens.ap().tensor, parts[0] * 0 + base,
                       [[tens.shape[1], parts[1]], [2, cnt]])
    for part in range(2):
        cbase = part * 1024
        for k in range(KT):
            wb = k * 3072 + cbase
            We = bass.AP(WA.ap().tensor, wb, [[KT * 3072, 128], [2, 512]])
            Wo = bass.AP(WA.ap().tensor, wb + 1, [[KT * 3072, 128], [2, 512]])
            with nc.Block() as blk:
                @blk.vector
                def _(v):
                    v.tensor_tensor(SCR[:, 0:512], We, CSE[:, 0:512], ALU.mult)
                    v.tensor_tensor(SCR[:, 512:1024], Wo, CSE[:, 512:1024], ALU.mult)
                    v.tensor_tensor(SCR[:, 1024:1536], Wo, CSE[:, 0:512], ALU.mult)
                    v.tensor_tensor(SCR[:, 1536:2048], We, CSE[:, 512:1024], ALU.mult)
            with nc.Block() as blk:
                @blk.vector
                def _(v):
                    v.tensor_sub(We, SCR[:, 0:512], SCR[:, 512:1024])
                    v.tensor_add(Wo, SCR[:, 1024:1536], SCR[:, 1536:2048])
        be = bass.AP(BR.ap().tensor, cbase, [[4104, 1], [2, 512]])
        bo = bass.AP(BR.ap().tensor, cbase + 1, [[4104, 1], [2, 512]])
        with nc.Block() as blk:
            @blk.vector
            def _(v):
                v.tensor_tensor(SCR[0:1, 0:512], be, CSE[0:1, 0:512], ALU.mult)
                v.tensor_tensor(SCR[0:1, 512:1024], bo, CSE[0:1, 512:1024], ALU.mult)
                v.tensor_tensor(SCR[0:1, 1024:1536], bo, CSE[0:1, 0:512], ALU.mult)
                v.tensor_tensor(SCR[0:1, 1536:2048], be, CSE[0:1, 512:1024], ALU.mult)
        with nc.Block() as blk:
            @blk.vector
            def _(v):
                v.tensor_sub(be, SCR[0:1, 0:512], SCR[0:1, 512:1024])
                v.tensor_add(bo, SCR[0:1, 1024:1536], SCR[0:1, 1536:2048])

    p.pop()
    SH = 0.125
    for h in range(16):
        for dst, cbase, sc in [(QA, h * 64, 1.0), (KA, 1024 + h * 64, SH)]:
            p.push()
            jobs = []
            for n0 in range(0, T, NCHUNK):
                mms = [(WA[:, k * 3072 + cbase: k * 3072 + cbase + 64],
                        XN[:, k * T + n0: k * T + n0 + NCHUNK]) for k in range(KT)]
                mms.append((BR[0:1, cbase:cbase + 64], ONR[0:1, 0:NCHUNK]))
                jobs.append(dict(
                    shape=(64, NCHUNK), mms=mms, evac_eng="scalar",
                    epi=(lambda eng, pa, n0=n0, dst=dst, sc=sc:
                         eng.activation(dst[0:64, n0:n0 + NCHUNK], pa, AF.Copy,
                                        scale=sc))))
            emit_mm(p, jobs, uname="a2p")
            p.pop()
        p.push()
        jobs = []
        for tt in range(16):
            mms = [(XN[:, k * T + tt * 128: k * T + (tt + 1) * 128],
                    WA[:, k * 3072 + 2048 + h * 64: k * 3072 + 2048 + h * 64 + 64])
                   for k in range(KT)]
            mms.append((ONR[0:1, 0:128], BR[0:1, 2048 + h * 64: 2048 + h * 64 + 64]))
            jobs.append(dict(
                shape=(128, 64), mms=mms, evac_eng="scalar",
                epi=(lambda eng, pa, tt=tt:
                     eng.activation(VT[:, tt * 64:(tt + 1) * 64], pa, AF.Copy))))
        emit_mm(p, jobs, uname="a2v")
        p.pop()

        emit_cbound(p, QA, KA, "a2c", SH)

        def write_out(eng, ot, qc, h=h):
            base = (h % 2) * 64
            j = h // 2
            return eng.tensor_copy(ATTO[base:base + 64,
                                        j * T + qc * 512: j * T + (qc + 1) * 512], ot)

        attn_core(p, QA, KA, VT,
                  write_out, list(range(4)),
                  lambda qc: list(range(4 * qc + 4)),
                  lambda qc, tk, : (MSK[:, (tk - 4 * qc) * 512:(tk - 4 * qc + 1) * 512]
                                    if tk >= 4 * qc else None),
                  uname="a2")

    p.push()
    WP = p.ssb("a2_wp", [128, 1024], BF16)
    for dt in range(KT):
        with nc.Block() as blk:
            @blk.gpsimd
            def _(g):
                gd.dma(g.dma_start(WP[:, :], W["a2_WprojC"][:, dt * 1024:(dt + 1) * 1024]))
                g.wait_ge(gd.h, gd.n)
        p.push()
        jobs = []
        for n0 in range(0, T, NCHUNK):
            mms = [(WP[:, k * 128:(k + 1) * 128],
                    ATTO[:, k * T + n0: k * T + n0 + NCHUNK]) for k in range(KT)]
            mms.append((BR[0:1, 3072 + dt * 128: 3072 + (dt + 1) * 128],
                        ONR[0:1, 0:NCHUNK]))
            jobs.append(dict(
                shape=(128, NCHUNK), mms=mms,
                epi=(lambda eng, pa, n0=n0, dt=dt:
                     eng.tensor_tensor(X[:, dt * T + n0: dt * T + n0 + NCHUNK],
                                       X[:, dt * T + n0: dt * T + n0 + NCHUNK],
                                       pa, ALU.add))))
        emit_mm(p, jobs, uname="a2o")
        p.pop()
    p.pop()
    p.pop()



def emit_l3_attn(p, X, W):
    nc = p.nc
    gd = p.sem("gdma")
    NF = 640
    p.push()
    XN = p.ssb("a3_XN", [128, KT * T], BF16)
    emit_ln(p, X, XN)

    W3 = p.ssb("a3_w3", [128, 8 * 1024], BF16)
    DCS = p.ssb("a3_dcs", [128, 8 * 128], BF16)
    DSS = p.ssb("a3_dss", [128, 8 * 128], BF16)
    CARRY = p.ssb("a3_carry", [128, 10], F32)
    QF = p.ssb("a3_qf", [128, KT * 1024], BF16)
    KF = p.ssb("a3_kf", [128, KT * 1024], BF16)
    VF = p.ssb("a3_vf", [128, KT * 1024], BF16)
    WTR = p.ssb("a3_wtr", [128, 5 * 1024], BF16)
    WTI = p.ssb("a3_wti", [128, 5 * 1024], BF16)
    SC1 = p.ssb("a3_sc1", [128, 1024], BF16)
    SC2 = p.ssb("a3_sc2", [128, 1024], BF16)
    SC3 = p.ssb("a3_sc3", [128, 1024], BF16)
    ONEB = p.ssb("a3_oneb", [128, 1024], BF16)
    FT = p.ssb("a3_ft", [128, 4 * 1024], BF16)

    with nc.Block() as blk:
        @blk.vector
        def _(v):
            v.memset(ONEB[:, :], 1.0)
            v.memset(CARRY[:, :], 0.0)

    def load_w3(name, nk=KT, wd=1024):
        with nc.Block() as blk:
            @blk.gpsimd
            def _(g):
                for k in range(nk):
                    gd.dma(g.dma_start(W3[:, k * wd:(k + 1) * wd],
                                       W[name][k * 128:(k + 1) * 128, :]))
                g.wait_ge(gd.h, gd.n)

    def load_dft(mt):
        with nc.Block() as blk:
            @blk.gpsimd
            def _(g):
                for name, dst in [("a3_DC", DCS), ("a3_DS", DSS)]:
                    srca = bass.AP(W[name].ap().tensor, mt * 128,
                                   [[NF, 128], [128 * NF, 8], [1, 128]])
                    gd.dma(g.dma_start(dst[:, :], srca))
                g.wait_ge(gd.h, gd.n)

    for tc in range(2):
        t0 = tc * 1024
        for name, dst in [("a3_WkT", KF), ("a3_WvT", VF), ("a3_WqT", QF)]:
            load_w3(name)
            p.push()
            emit_mm(p, proj_jobs(
                lambda k, m0, msz: W3[:, k * 1024 + m0: k * 1024 + m0 + msz],
                lambda k, n0, nsz: XN[:, k * T + t0 + n0: k * T + t0 + n0 + nsz],
                lambda eng, pa, m0, msz, n0, nsz, dst=dst:
                eng.activation(dst[:, (m0 // 128) * 1024 + n0:
                                   (m0 // 128) * 1024 + n0 + nsz], pa, AF.Copy),
                D, ncols=1024, evac_eng="scalar"), uname="a3p")
            p.pop()

        for mt in range(5):
            load_dft(mt)
            jobs = []
            for (srcb, off) in [(KF, 0), (VF, 2)]:
                for comp, tab in [(0, DCS), (1, DSS)]:
                    for n0 in range(0, 1024, NCHUNK):
                        jobs.append(dict(
                            shape=(128, NCHUNK),
                            mms=[(tab[:, k * 128:(k + 1) * 128],
                                  srcb[:, k * 1024 + n0: k * 1024 + n0 + NCHUNK])
                                 for k in range(KT)],
                            epi=(lambda eng, pa, off=off, comp=comp, n0=n0:
                                 eng.activation(FT[:, (off + comp) * 1024 + n0:
                                                   (off + comp) * 1024 + n0 + NCHUNK],
                                                pa, AF.Copy)),
                            evac_eng="scalar"))
            p.push()
            emit_mm(p, jobs, uname="a3f", n_pp=4)
            p.pop()
            with nc.Block() as blk:
                @blk.vector
                def _(v):
                    kre, kim = FT[:, 0:1024], FT[:, 1024:2048]
                    vre, vim = FT[:, 2048:3072], FT[:, 3072:4096]
                    v.tensor_tensor(SC1[:, :], kre, vre, ALU.mult)
                    v.tensor_tensor(SC2[:, :], kim, vim, ALU.mult)
                    v.tensor_sub(WTR[:, mt * 1024:(mt + 1) * 1024], SC1[:, :], SC2[:, :])
                    v.tensor_tensor(SC1[:, :], kre, vim, ALU.mult)
                    v.tensor_tensor(SC2[:, :], kim, vre, ALU.mult)
                    v.tensor_add(WTI[:, mt * 1024:(mt + 1) * 1024], SC1[:, :], SC2[:, :])

        for mt in range(5):
            load_dft(mt)
            jobs = []
            for comp, tab in [(0, DCS), (1, DSS)]:
                for n0 in range(0, 1024, NCHUNK):
                    jobs.append(dict(
                        shape=(128, NCHUNK),
                        mms=[(tab[:, k * 128:(k + 1) * 128],
                              QF[:, k * 1024 + n0: k * 1024 + n0 + NCHUNK])
                             for k in range(KT)],
                        epi=(lambda eng, pa, comp=comp, n0=n0:
                             eng.activation(FT[:, comp * 1024 + n0:
                                               comp * 1024 + n0 + NCHUNK], pa, AF.Copy)),
                        evac_eng="scalar"))
            p.push()
            emit_mm(p, jobs, uname="a3f", n_pp=4)
            p.pop()
            with nc.Block() as blk:
                @blk.vector
                def _(v):
                    v.tensor_tensor_scan(SC1[:, :], ONEB[:, :],
                                         WTR[:, mt * 1024:(mt + 1) * 1024],
                                         CARRY[:, mt:mt + 1], ALU.mult, ALU.add)
                    v.tensor_tensor_scan(SC2[:, :], ONEB[:, :],
                                         WTI[:, mt * 1024:(mt + 1) * 1024],
                                         CARRY[:, 5 + mt:6 + mt], ALU.mult, ALU.add)
            with nc.Block() as blk:
                @blk.vector
                def _(v):
                    v.tensor_copy(CARRY[:, mt:mt + 1], SC1[:, 1023:1024])
                    v.tensor_copy(CARRY[:, 5 + mt:6 + mt], SC2[:, 1023:1024])
            with nc.Block() as blk:
                @blk.vector
                def _(v):
                    qre, qim = FT[:, 0:1024], FT[:, 1024:2048]
                    v.tensor_tensor(SC3[:, :], SC1[:, :], qre, ALU.mult)
                    v.tensor_tensor(FT[:, 2048:3072], SC2[:, :], qim, ALU.mult)
                    v.tensor_tensor(FT[:, 3072:4096], SC2[:, :], qre, ALU.mult)
                    v.tensor_tensor(SC1[:, :], SC1[:, :], qim, ALU.mult)
            with nc.Block() as blk:
                @blk.vector
                def _(v):
                    v.tensor_add(WTR[:, mt * 1024:(mt + 1) * 1024], SC3[:, :],
                                 FT[:, 2048:3072])
                    v.tensor_sub(WTI[:, mt * 1024:(mt + 1) * 1024],
                                 FT[:, 3072:4096], SC1[:, :])

        for name, srcb in [("a3_IC", WTR), ("a3_IS", WTI)]:
            load_w3(name, nk=5)
            p.push()
            jobs = []
            for dt in range(KT):
                for n0 in range(0, 1024, NCHUNK):
                    mms = [(W3[:, k * 1024 + dt * 128: k * 1024 + (dt + 1) * 128],
                            srcb[:, k * 1024 + n0: k * 1024 + n0 + NCHUNK])
                           for k in range(5)]
                    jobs.append(dict(
                        shape=(128, NCHUNK), mms=mms,
                        epi=(lambda eng, pa, dt=dt, n0=n0:
                             eng.tensor_tensor(
                                 X[:, dt * T + t0 + n0: dt * T + t0 + n0 + NCHUNK],
                                 X[:, dt * T + t0 + n0: dt * T + t0 + n0 + NCHUNK],
                                 pa, ALU.add))))
            emit_mm(p, jobs, uname="a3o")
            p.pop()
    p.pop()


def emit_l3_ffn(p, X, W):
    nc = p.nc
    HID = 3072
    XN = p.sb("XN", [128, KT * T], BF16)
    emit_ln(p, X, XN)

    p.push()
    H = p.ssb("l3f_H", [128, 24 * 1024], BF16)
    GA = p.ssb("l3f_G", [128, 2 * 512], BF16)
    WB = p.ssb("l3f_W", [128, 4 * 1024], BF16)
    gd = p.sem("gdma")

    for tc in range(2):
        t0 = tc * 1024
        jobs = []
        load_groups = []
        base_gd = gd.n
        nmt = HID // 128
        frv_now = p.sem("frv_l3f").n
        frs_now = p.sem("frs_l3f").n
        for mt in range(nmt):
            slot = mt % 2
            need = (4 * (mt - 1)) if mt >= 2 else None
            load_groups.append((need, [
                (WB[:, slot * 2048: slot * 2048 + 1024],
                 W["wgT_c"][:, mt * 1024:(mt + 1) * 1024]),
                (WB[:, slot * 2048 + 1024: slot * 2048 + 2048],
                 W["wuT_c"][:, mt * 1024:(mt + 1) * 1024]),
            ]))
            for half in range(2):
                n0 = t0 + half * 512

                def g_write(eng, pa, half=half):
                    return eng.activation(GA[:, half * 512:(half + 1) * 512], pa,
                                          AF.Silu, scale=1.0 / 127.0)

                def u_write(eng, pa, mt=mt, half=half):
                    return eng.scalar_tensor_tensor(
                        H[:, mt * 1024 + half * 512: mt * 1024 + (half + 1) * 512],
                        pa, 1.0 / 127.0, GA[:, half * 512:(half + 1) * 512],
                        ALU.mult, ALU.mult)

                ws = slot * 2048
                gate_ew = ([("frv_l3f", frv_now + 2 * (mt - 1) + half + 1)]
                           if mt >= 1 else None)
                jobs.append(dict(
                    shape=(128, 512), evac_eng="scalar",
                    mms=[(WB[:, ws + k * 128: ws + (k + 1) * 128],
                          XN[:, k * T + n0: k * T + n0 + 512]) for k in range(KT)],
                    epi=g_write,
                    wait=("gdma", base_gd + 32 * (mt + 1)),
                    evac_waits=gate_ew,
                ))
                jobs.append(dict(
                    shape=(128, 512), evac_eng="vector",
                    mms=[(WB[:, ws + 1024 + k * 128: ws + 1024 + (k + 1) * 128],
                          XN[:, k * T + n0: k * T + n0 + 512]) for k in range(KT)],
                    epi=u_write,
                    evac_waits=[("frs_l3f", frs_now + 2 * mt + half + 1)],
                ))
        p.push()
        emit_mm(p, jobs, uname="l3f", load_groups=load_groups, n_pp=4)
        p.pop()

        for half in range(2):
            n0 = t0 + half * 512
            p.push()
            WD = p.ssb("l3f_WD", [128, 2 * 1024], BF16)
            pss = [p.sps(f"ps_dn{i}", [128, 512], F32) for i in range(8)]
            pes = p.sem("pes_dn")
            base_pe = pes.n
            base_gd2 = gd.n
            with nc.Block() as blk:
                @blk.gpsimd
                def _(g):
                    for k in range(24):
                        if k >= 2:
                            g.wait_ge(pes.h, base_pe + (k - 1))
                        gd.dma(g.dma_start(WD[:, (k % 2) * 1024:(k % 2 + 1) * 1024],
                                           W["wdT"][k * 128:(k + 1) * 128, :]))
                    g.wait_ge(gd.h, gd.n)

                @blk.tensor
                def _(t):
                    for k in range(24):
                        t.wait_ge(gd.h, base_gd2 + 16 * (k + 1))
                        ins = None
                        for dt in range(KT):
                            ins = t.matmul(pss[dt][:, :],
                                           WD[:, (k % 2) * 1024 + dt * 128:
                                              (k % 2) * 1024 + (dt + 1) * 128],
                                           H[:, k * 1024 + half * 512:
                                             k * 1024 + (half + 1) * 512],
                                           start=(k == 0), stop=(k == 23))
                        pes.inc(ins)

                @blk.vector
                def _(v):
                    v.wait_ge(pes.h, base_pe + 24)
                    for dt in range(KT):
                        v.scalar_tensor_tensor(
                            X[:, dt * T + n0: dt * T + n0 + 512],
                            pss[dt][:, :], 1.0 / 127.0,
                            X[:, dt * T + n0: dt * T + n0 + 512],
                            ALU.mult, ALU.add)
            p.pop()
    p.pop()



def chunkify_km(wt, hid):
    out = np.empty((128, (hid // 128) * 1024), wt.dtype)
    for mt in range(hid // 128):
        for k in range(KT):
            out[:, mt * 1024 + k * 128: mt * 1024 + (k + 1) * 128] = \
                wt[k * 128:(k + 1) * 128, mt * 128:(mt + 1) * 128]
    return _cc(out)


def host_prep(inputs):
    W = {}
    W["c_ones1"] = np.ones((1, 128), np.float32)
    W["c_ones128"] = np.ones((128, 1), np.float32)
    W["l0_WxT"] = _cc(np.asarray(inputs["L0_attn_Wx"]).T)
    W["l0_WzT"] = _cc(np.asarray(inputs["L0_attn_Wz"]).T)
    W["l0_WoT"] = _cc(np.asarray(inputs["L0_attn_Wo"]).T)
    W["l0_WpT"] = _cc(np.asarray(inputs["L0_attn_Wp"]).T)
    W["l0_WdtT"] = _cc(np.asarray(inputs["L0_attn_Wdt"]).T)
    W["l0_cw_t"] = _cc(np.asarray(inputs["L0_attn_conv_w"]).reshape(8, 128, 4)
                       .transpose(1, 0, 2).reshape(128, 32))
    W["l0_bdt_t"] = _cc(np.asarray(inputs["L0_attn_bdt"]).reshape(8, 128).T)
    W["l0_Dp_t"] = _cc(np.asarray(inputs["L0_attn_Dp"]).reshape(8, 128).T)
    W["l0_A_t"] = _cc(np.asarray(inputs["L0_attn_A_log"]).reshape(8, 128, 16)
                      .transpose(1, 0, 2).reshape(128, 128))
    W["c_eye"] = np.eye(128, dtype=np.float32)
    wq1 = np.asarray(inputs["L1_attn_Wq"])
    wk1 = np.asarray(inputs["L1_attn_Wk"])
    wv1 = np.asarray(inputs["L1_attn_Wv"])
    for h in range(16):
        W[f"a1_wh_{h}"] = _cc(np.concatenate([wq1[h], wk1[h], wv1[h]], axis=1))
    brow = np.concatenate(
        [np.concatenate([np.asarray(inputs["L1_attn_bq"])[h],
                         np.asarray(inputs["L1_attn_bk"])[h],
                         np.asarray(inputs["L1_attn_bv"])[h]]) for h in range(16)]
        + [np.asarray(inputs["L1_attn_bo"])])
    W["a1_brow"] = _cc(brow.reshape(1, -1).astype(np.float32))
    W["a1_lam"] = _cc(np.asarray(inputs["L1_attn_lam"]).reshape(1, 1).astype(np.float32))
    W["a1_WoC"] = chunkify_km(_cc(np.asarray(inputs["L1_attn_Wo"])), 1024)
    W["a1_gamma_t"] = _cc(np.asarray(inputs["L1_attn_gamma"]).reshape(8, 128).T)
    W["a1_beta_t"] = _cc(np.asarray(inputs["L1_attn_beta"]).reshape(8, 128).T)
    wqkv6 = np.asarray(inputs["L1_ffn_Wqkv"])
    wo6 = np.asarray(inputs["L1_ffn_Wo"])
    w16 = np.asarray(inputs["L1_ffn_W1"])
    b16 = np.asarray(inputs["L1_ffn_b1"])
    w26 = np.asarray(inputs["L1_ffn_W2"])
    b26 = np.asarray(inputs["L1_ffn_b2"])
    for i in range(6):
        W[f"f1_wqkv_{i}"] = _cc(wqkv6[i])
        W[f"f1_wo_{i}"] = chunkify_km(_cc(wo6[i]), 1024)
        w1a = np.zeros((1024, 2816), np.float32); w1a[:, :2730] = w16[i][:, :2730]
        w1g = np.zeros((1024, 2816), np.float32); w1g[:, :2730] = w16[i][:, 2730:]
        W[f"f1_w1a_{i}"] = chunkify_km(w1a, 2816)
        W[f"f1_w1g_{i}"] = chunkify_km(w1g, 2816)
        w2p = np.zeros((2816, 1024), np.float32); w2p[:2730] = w26[i]
        W[f"f1_w2_{i}"] = _cc(w2p)
        b1a = np.zeros(2816, np.float32); b1a[:2730] = b16[i][:2730]
        b1g = np.zeros(2816, np.float32); b1g[:2730] = b16[i][2730:]
        W[f"f1_b1a_{i}"] = _cc(b1a.reshape(22, 128).T)
        W[f"f1_b1g_{i}"] = _cc(b1g.reshape(22, 128).T)
        W[f"f1_b2_{i}"] = _cc(b26[i].reshape(8, 128).T)
    half = np.arange(0, 64, 2, dtype=np.float64)
    inv_freq = 1.0 / 10000.0 ** (half / 64.0)
    tt_ = np.arange(2048, dtype=np.float64)
    fr = tt_[:, None] * inv_freq[None, :]
    frc = np.concatenate([fr, fr], 1)
    base_scale = (half + 0.4 * 64) / (1.4 * 64)
    power = (tt_ - 1024) / 128.0
    sc = base_scale[None, :] ** power[:, None]
    scc = np.concatenate([sc, sc], 1)
    sgn = np.concatenate([-np.ones(32), np.ones(32)])[None, :]
    W["f1_csq"] = _cc((np.cos(frc) * scc).T.astype(np.float32))
    ssq_ = (np.sin(frc) * scc * sgn).T.astype(np.float32)
    W["f1_ssq"] = _cc(np.concatenate([ssq_[32:64], ssq_[0:32]], 0))
    W["f1_csk"] = _cc((np.cos(frc) / scc * 8.0).T.astype(np.float32))
    ssk_ = (np.sin(frc) / scc * sgn * 8.0).T.astype(np.float32)
    W["f1_ssk"] = _cc(np.concatenate([ssk_[32:64], ssk_[0:32]], 0))
    W["f1_qs"] = _cc(np.asarray(inputs["L1_ffn_q_scale"]).T)
    W["f1_ks"] = _cc(np.asarray(inputs["L1_ffn_k_scale"]).T)
    W["f1_lng"] = _cc(np.asarray(inputs["L1_ffn_ln_g"]).reshape(6, 8, 128)
                      .transpose(2, 0, 1).reshape(128, 48))
    W["f1_lnb"] = _cc(np.asarray(inputs["L1_ffn_ln_b"]).reshape(6, 8, 128)
                      .transpose(2, 0, 1).reshape(128, 48))
    mm_ = np.arange(128)[:, None]; pp_ = np.arange(128)[None, :]
    up = (mm_ >= pp_).astype(np.float32); lo = (mm_ <= pp_).astype(np.float32)
    z = np.zeros((128, 128), np.float32); fu = np.ones((128, 128), np.float32)
    W["f1_mskl"] = _cc(np.concatenate(
        [np.concatenate([up, z], 1), np.concatenate([fu, up], 1),
         np.concatenate([lo, fu], 1), np.concatenate([z, lo], 1)], 1))
    W["a2_Wattn"] = _cc(np.asarray(inputs["L2_attn_W_attn"]))
    W["a2_battn"] = _cc(np.concatenate([np.asarray(inputs["L2_attn_b_attn"]),
                                        np.asarray(inputs["L2_attn_b_proj"])]).reshape(1, 4096))
    W["a2_WprojC"] = chunkify_km(_cc(np.asarray(inputs["L2_attn_W_proj"])), 1024)
    hh = np.arange(16)[:, None]
    ii = np.arange(32)[None, :]
    th = hh * (10000.0 ** (-2.0 * ii / 64.0))
    W["a2_cose"] = _cc(np.broadcast_to(np.cos(th).reshape(1, 512), (128, 512)).astype(np.float32))
    W["a2_sine"] = _cc(np.broadcast_to(np.sin(th).reshape(1, 512), (128, 512)).astype(np.float32))
    msk = np.zeros((128, 4 * 512), np.float32)
    for r in range(4):
        msk[:, r * 512:(r + 1) * 512] = (r * 128 + np.arange(128)[:, None]
                                         <= np.arange(512)[None, :])
    W["a2_mskc"] = _cc(msk)
    e_ = np.arange(1024)[:, None]
    f_ = np.arange(640)[None, :]
    valid = (f_ < 513).astype(np.float64)
    ang = 2.0 * np.pi * e_ * f_ / 1024.0
    W["a3_DC"] = _cc((np.cos(ang) * valid).astype(np.float32))
    W["a3_DS"] = _cc((-np.sin(ang) * valid).astype(np.float32))
    wf = np.where((f_ == 0) | (f_ == 512), 1.0, 2.0) / 1024.0 * valid
    W["a3_IC"] = _cc((np.cos(ang) * wf).T.astype(np.float32))
    W["a3_IS"] = _cc((-np.sin(ang) * wf).T.astype(np.float32))
    W["a3_WqT"] = _cc(np.asarray(inputs["L3_attn_Wq"]).T)
    W["a3_WkT"] = _cc(np.asarray(inputs["L3_attn_Wk"]).T)
    W["a3_WvT"] = _cc(np.asarray(inputs["L3_attn_Wv"]).T)
    W["f2_wfc"] = _cc(np.asarray(inputs["L2_ffn_w_fc"]))
    W["f2_wproj"] = _cc(np.asarray(inputs["L2_ffn_w_proj"]))
    W["f2_vT"] = _cc(np.asarray(inputs["L2_ffn_v"]).T)
    W["f2_fbT"] = _cc(np.asarray(inputs["L2_ffn_fast_b"]).T)
    W["f2_u"] = _cc(np.asarray(inputs["L2_ffn_u"]))
    W["f2_gate"] = _cc(np.asarray(inputs["L2_ffn_gate"]).reshape(1, 1).astype(np.float32))
    W["f0_gwT"] = _cc(np.asarray(inputs["L0_ffn_gate_w"]).T)
    w1 = np.asarray(inputs["L0_ffn_w1"])
    w2 = np.asarray(inputs["L0_ffn_w2"])
    for e in range(8):
        W[f"f0_w1T_{e}"] = chunkify_km(_cc(w1[e].T), 4096)
        W[f"f0_w2T_{e}"] = _cc(w2[e].T)
    W["wgT_c"] = chunkify_km(_cc(np.asarray(inputs["L3_ffn_w_gate"]).T), 3072)
    W["wuT_c"] = chunkify_km(_cc(np.asarray(inputs["L3_ffn_w_up"]).T), 3072)
    W["wdT"] = _cc(np.asarray(inputs["L3_ffn_w_down"]).T)
    return W


DT_MAP = {np.dtype(np.float32): F32, np.dtype(np.int32): I32}

SLOT_EMITTERS = {}
SLOT_WEIGHTS = {
    "l0_attn": {"l0_WxT", "l0_WzT", "l0_WoT", "l0_WpT", "l0_WdtT", "l0_cw_t",
                "l0_bdt_t", "l0_Dp_t", "l0_A_t"},
    "l0_ffn": {"c_eye", "f0_gwT"} | {f"f0_w1T_{e}" for e in range(8)}
              | {f"f0_w2T_{e}" for e in range(8)},
    "l1_attn": {f"a1_wh_{h}" for h in range(16)} | {"a1_brow", "a1_lam", "a1_WoC",
                "a1_gamma_t", "a1_beta_t"},
    "l1_ffn": ({f"f1_wqkv_{i}" for i in range(6)} | {f"f1_wo_{i}" for i in range(6)}
               | {f"f1_w1a_{i}" for i in range(6)} | {f"f1_w1g_{i}" for i in range(6)}
               | {f"f1_w2_{i}" for i in range(6)} | {f"f1_b1a_{i}" for i in range(6)}
               | {f"f1_b1g_{i}" for i in range(6)} | {f"f1_b2_{i}" for i in range(6)}
               | {"f1_csq", "f1_ssq", "f1_csk", "f1_ssk", "f1_qs", "f1_ks",
                  "f1_lng", "f1_lnb", "f1_mskl"}),
    "l2_attn": {"a2_Wattn", "a2_battn", "a2_WprojC", "a2_cose", "a2_sine", "a2_mskc"},
    "l3_attn": {"a3_DC", "a3_DS", "a3_IC", "a3_IS", "a3_WqT", "a3_WkT", "a3_WvT"},
    "l2_ffn": {"c_eye", "f2_wfc", "f2_wproj", "f2_vT", "f2_fbT", "f2_u", "f2_gate"},
    "l3_ffn": {"wgT_c", "wuT_c", "wdT"},
}


def build_program(slots, W):
    p = Prog()
    x_in = p.dram("x_in", [D, T], F32, "ExternalInput")
    x_out = p.dram("x_out", [D, T], F32, "ExternalOutput")
    used = set()
    for s in slots:
        used |= SLOT_WEIGHTS[s]
    names = sorted(used | {"c_eye"}) + ["c_ones1", "c_ones128"]
    waps = {n: p.dram(n, list(W[n].shape), DT_MAP[W[n].dtype], "ExternalInput")
            for n in names}
    X = p.sb("X", [128, KT * T], F32)
    emit_load_consts(p, waps)
    emit_eye(p, waps)
    emit_load_x(p, X, x_in)
    for s in slots:
        SLOT_EMITTERS[s](p, X, waps)
    emit_store_x(p, X, x_out)
    p.close()
    return p, names


SLOT_EMITTERS["l0_attn"] = emit_l0_attn
SLOT_EMITTERS["l0_ffn"] = emit_l0_ffn
SLOT_EMITTERS["l1_attn"] = emit_l1_attn
SLOT_EMITTERS["l2_attn"] = emit_l2_attn
SLOT_EMITTERS["l3_attn"] = emit_l3_attn
SLOT_EMITTERS["l2_ffn"] = emit_l2_ffn
SLOT_EMITTERS["l3_ffn"] = emit_l3_ffn
ALL_SLOTS = ["l0_attn", "l0_ffn", "l1_attn", "l2_attn", "l2_ffn", "l3_attn", "l3_ffn"]


def build_l1ffn(kind, li, W):
    p = Prog()
    names = sorted(SLOT_WEIGHTS["l1_ffn"] | {"c_eye"}) + ["c_ones1", "c_ones128"]
    waps = {n: p.dram(n, list(W[n].shape), DT_MAP[W[n].dtype], "ExternalInput")
            for n in names}
    sd = p.sem("sdma")
    gd = p.sem("gdma")
    nc = p.nc
    emit_load_consts(p, waps)
    emit_eye(p, waps)
    if kind == "ln":
        x_in = p.dram("x_in", [D, T], F32, "ExternalInput")
        y_out = p.dram("y_out", [D, T], F32, "ExternalOutput")
        X = p.sb("X", [128, KT * T], F32)
        emit_load_x(p, X, x_in)
        p.push()
        Y = p.ssb("f1_Y", [128, KT * T], BF16)
        emit_ln(p, X, Y)
        with nc.Block() as blk:
            @blk.gpsimd
            def _(g):
                for j in range(KT):
                    gd.dma(g.dma_start(y_out[j * 128:(j + 1) * 128, :],
                                       Y[:, j * T:(j + 1) * T]))
                g.wait_ge(gd.h, gd.n)
        p.pop()
        p.close()
        return p, names, ["x_in"], "y_out"
    y_in = p.dram("y_in", [D, T], F32, "ExternalInput")
    Y = p.sb("Yg", [128, KT * T], BF16)
    with nc.Block() as blk:
        @blk.gpsimd
        def _(g):
            for j in range(KT):
                gd.dma(g.dma_start(Y[:, j * T:(j + 1) * T],
                                   y_in[j * 128:(j + 1) * 128, :]))
            g.wait_ge(gd.h, gd.n)
    if kind == "mid":
        y_out = p.dram("y_out", [D, T], F32, "ExternalOutput")
        emit_l1_ffn_layer(p, Y, waps, li)
        with nc.Block() as blk:
            @blk.gpsimd
            def _(g):
                for j in range(KT):
                    gd.dma(g.dma_start(y_out[j * 128:(j + 1) * 128, :],
                                       Y[:, j * T:(j + 1) * T]))
                g.wait_ge(gd.h, gd.n)
        p.close()
        return p, names, ["y_in"], "y_out"
    x_in = p.dram("x_in", [D, T], F32, "ExternalInput")
    x_out = p.dram("x_out", [D, T], F32, "ExternalOutput")
    emit_l1_ffn_layer(p, Y, waps, li)
    X = p.sb("X", [128, KT * T], F32)
    emit_load_x(p, X, x_in)
    with nc.Block() as blk:
        @blk.vector
        def _(v):
            for j in range(KT):
                v.tensor_tensor(X[:, j * T:(j + 1) * T], X[:, j * T:(j + 1) * T],
                                Y[:, j * T:(j + 1) * T], ALU.add)
    emit_store_x(p, X, x_out)
    p.close()
    return p, names, ["y_in", "x_in"], "x_out"


def run_l1ffn(x, W, cores=8):
    import gc
    names = sorted(SLOT_WEIGHTS["l1_ffn"] | {"c_eye"}) + ["c_ones1", "c_ones128"]
    wmap = {"hb_" + n: W[n] for n in names}
    p, _, _, _ = build_l1ffn("ln", 0, W)
    res = run_bass_kernel_spmd(p.nc, [dict({"hb_x_in": _cc(x[c])}, **wmap)
                                      for c in range(cores)],
                               core_ids=list(range(cores)))
    y = np.stack([np.asarray(r["hb_y_out"]) for r in res.results])
    del res, p
    gc.collect()
    for li in range(5):
        p, _, _, _ = build_l1ffn("mid", li, W)
        res = run_bass_kernel_spmd(p.nc, [dict({"hb_y_in": _cc(y[c])}, **wmap)
                                          for c in range(cores)],
                                   core_ids=list(range(cores)))
        y = np.stack([np.asarray(r["hb_y_out"]) for r in res.results])
        del res, p
        gc.collect()
    p, _, _, _ = build_l1ffn("last", 5, W)
    res = run_bass_kernel_spmd(p.nc, [dict({"hb_y_in": _cc(y[c]),
                                            "hb_x_in": _cc(x[c])}, **wmap)
                                      for c in range(cores)],
                               core_ids=list(range(cores)))
    out = np.stack([np.asarray(r["hb_x_out"]) for r in res.results])
    del res, p
    gc.collect()
    return out


def run_slots(slots, x, W, cores=8):
    import gc
    p, names = build_program(slots, W)
    in_maps = [dict({"hb_x_in": _cc(x[c])}, **{"hb_" + n: W[n] for n in names})
               for c in range(cores)]
    res = run_bass_kernel_spmd(p.nc, in_maps, core_ids=list(range(cores)))
    out = np.stack([np.asarray(r["hb_x_out"]) for r in res.results])
    del res, in_maps, p
    gc.collect()
    return out


ALL_SLOTS = ["l0_attn", "l0_ffn", "l1_attn", "l1_ffn", "l2_attn", "l2_ffn",
             "l3_attn", "l3_ffn"]


def kernel(**inputs):
    x = np.asarray(inputs["L0_attn_x"])
    xf = _cc(np.transpose(x, (0, 2, 1)))
    W = host_prep(inputs)
    for s in ALL_SLOTS:
        if s == "l1_ffn":
            xf = run_l1ffn(xf, W)
        else:
            xf = run_slots([s], xf, W)
    return _cc(np.transpose(xf, (0, 2, 1)).astype(np.float32))
